# version 02
# baseline (speedup 1.0000x reference)
import numpy as np
import concourse.bass as bass
import concourse.mybir as mybir
from concourse.bass_utils import run_bass_kernel_spmd

F32 = mybir.dt.float32
AF = mybir.ActivationFunctionType
ALU = mybir.AluOpType
AX = mybir.AxisListType

D = 1024
DI = 2048
DIH = 1024
NST = 16
DTR = 64
NE = 15
HF = 512
DC = 8
MI = 8
HC = HF // 128
XD = DTR + 2 * NST
EPS = 1e-5

ENGS = ("sy", "t", "v", "s", "g")
DMA_LANES = {"sy": ("ds0", "ds1", "ds2", "ds3"), "g": ("dg0", "dg1")}
ALL_SEMS = ("ds0", "ds1", "ds2", "ds3", "dg0", "dg1", "t", "v", "s", "cc")


class Sched:

    def __init__(self):
        self.ops = {e: [] for e in ENGS}
        self.cnt = {k: 0 for k in ALL_SEMS}
        self.lastw = {}
        self.rr = {"sy": 0, "g": 0}

    def emit(self, eng, fn, waits=(), sem="default", n=None):
        waits = [w for w in waits if w is not None]
        if sem == "default":
            if eng in DMA_LANES:
                lanes = DMA_LANES[eng]
                sem = lanes[self.rr[eng] % len(lanes)]
                self.rr[eng] += 1
                n = 16
                waits = waits + [(sem, self.cnt[sem])]
            else:
                sem, n = {"t": ("t", 1), "v": ("v", 1), "s": ("s", 1)}[eng]
        elif sem == "cc":
            waits = waits + [("cc", self.cnt["cc"])]
        best = {}
        for w in waits:
            if w is None:
                continue
            s, v = w
            assert v <= self.cnt[s], f"wait on future {s}:{v} > {self.cnt[s]}"
            if best.get(s, 0) < v:
                best[s] = v
        rw = []
        for s, v in best.items():
            key = (eng, s)
            if self.lastw.get(key, 0) < v:
                self.lastw[key] = v
                rw.append((s, v))
        tag = None
        if sem is not None:
            self.cnt[sem] += n
            tag = (sem, self.cnt[sem])
        self.ops[eng].append((tuple(rw), fn, sem, n))
        return tag

    def now(self, sem):
        return (sem, self.cnt[sem])

    def barrier(self):
        snap = [(s, self.cnt[s]) for s in self.cnt if self.cnt[s] > 0]
        for e in ENGS:
            self.emit(e, lambda h: h.nop(), waits=snap, sem=None)


class Rot:

    def __init__(self, aps):
        self.aps = list(aps)
        self.deps = [[] for _ in self.aps]
        self.i = -1

    def acquire(self):
        self.i += 1
        s = self.i % len(self.aps)
        w = self.deps[s]
        self.deps[s] = []
        return s, self.aps[s], w

    def used(self, s, tag):
        if tag is not None:
            self.deps[s].append(tag)


def build_nc(TOK=4096, TC=512):
    NCH = TOK // TC
    NPASS = NCH // 2
    PT = 2 * TC
    NG = TC // 128

    nc = bass.Bass("TRN2", target_bir_lowering=False)

    def di(name, shape):
        return nc.dram_tensor(name, shape, F32, kind="ExternalInput").ap()

    xT = di("xT", [D, TOK])
    w_in = di("w_in", [D, DI])
    w_xp = di("w_xp", [DIH, XD])
    w_dt = di("w_dt", [DTR, DIH])
    dt_b = di("dt_b", [128, MI])
    cv_w = di("cv_w", [128, MI * 4])
    cv_b = di("cv_b", [128, MI])
    a_log = di("a_log", [128, MI * NST])
    d_vec = di("d_vec", [128, MI])
    nw = di("nw", [128, DC])
    w_out = di("w_out", [DIH, D])
    r_w = di("r_w", [D, NE])
    r_b = di("r_b", [NE, 1])
    sw1 = di("sw1", [D, HF])
    sb1 = di("sb1", [128, HC])
    sw2 = di("sw2", [HF, D])
    sb2 = di("sb2", [128, DC])
    e_w1 = di("e_w1", [NE, D, HF])
    e_b1 = di("e_b1", [128, NE * HC])
    e_w2 = di("e_w2", [NE, HF, D])
    e_b2 = di("e_b2", [NE, D])
    ones1 = di("ones1", [1, 128])
    onesd = di("onesd", [128, 1])
    id15 = di("id15", [NE, NE])
    id128 = di("id128", [128, 128])
    cst = di("cst", [128, 2])

    out = nc.dram_tensor("out", [D, TOK], F32, kind="ExternalOutput").ap()

    xd_in = nc.dram_tensor("xd_in", [NCH, XD, TC], F32).ap()
    xd_out = nc.dram_tensor("xd_out", [NCH, 2, XD, TC], F32).ap()
    xdf_d = nc.dram_tensor("xdf_d", [NCH, XD, TC], F32).ap()
    p_in = nc.dram_tensor("p_in", [NCH, D, TC], F32).ap()
    p_out = nc.dram_tensor("p_out", [NCH, 2, D, TC], F32).ap()
    cw_d = nc.dram_tensor("cw_d", [NPASS, NE, PT], F32).ap()
    rt_in = nc.dram_tensor("rt_in", [NPASS, D, PT], F32).ap()
    rt_out = nc.dram_tensor("rt_out", [NPASS, 2, D, PT], F32).ap()
    x1_d = nc.dram_tensor("x1_d", [NPASS, D, PT], F32).ap()

    S = Sched()
    RG = [[0, 1], [2, 3], [4, 5], [6, 7]]

    from contextlib import ExitStack
    with ExitStack() as _ctx:
        def sb(name, shape):
            return _ctx.enter_context(nc.sbuf_tensor(name, shape, F32))

        b_dtb = sb("b_dtb", [128, MI])
        b_cvw = sb("b_cvw", [128, MI * 4])
        b_cvb = sb("b_cvb", [128, MI])
        b_A = sb("b_A", [128, MI * NST])
        b_dv = sb("b_dv", [128, MI])
        b_nw = sb("b_nw", [128, DC])
        b_ones1 = sb("b_ones1", [1, 128])
        b_onesd = sb("b_onesd", [128, 1])
        b_id15 = sb("b_id15", [NE, NE])
        b_id128 = sb("b_id128", [128, 128])
        b_rb = sb("b_rb", [NE, 1])
        b_sb1 = sb("b_sb1", [128, HC])
        b_sb2h = sb("b_sb2h", [128, DC])
        b_eb1 = sb("b_eb1", [128, NE * HC])
        b_eb2h = sb("b_eb2h", [NE, D])
        b_wxp = sb("b_wxp", [128, MI * XD])
        b_wdt = sb("b_wdt", [DTR, DIH])
        b_rw = sb("b_rw", [128, DC * NE])
        b_state = sb("b_state", [128, MI * NST])
        b_halo = sb("b_halo", [128, MI * 3])
        b_rows = sb("b_rows", [1, 4 * TC])
        b_cst = sb("b_cst", [128, 2])
        bsq = sb("bsq", [128, 2 * TC])
        _mctx = ExitStack()
        def sbm(name, shape):
            return _mctx.enter_context(nc.sbuf_tensor(name, shape, F32))
        bx = sbm("bx", [128, DC * TC])
        bh = sbm("bh", [128, DC * TC])
        bu = sbm("bu", [128, MI * TC])
        bsr = sbm("bsr", [128, MI * TC])
        bdl = sbm("bdl", [128, MI * TC])
        bdu = sbm("bdu", [128, MI * TC])
        by = sbm("by", [128, MI * TC])
        bexp = sbm("bexp", [128, 3 * TC])
        bdbu = sbm("bdbu", [128, 4 * TC])
        bbcB = sbm("bbcB", [128, 2 * TC])
        bbcC = sbm("bbcC", [128, 2 * TC])
        cbuf = sbm("cbuf", [128, 2 * (TC + 6)])
        wbufA = sbm("wbufA", [128, 2 * 1024])
        xdl = sbm("xdl", [XD, TC])
        xdf = sbm("xdf", [XD, TC])
        xdf2 = sbm("xdf2", [XD, TC])
        p0, p1, p2, p3, p4, p5, p6, p7 = (
            _ctx.enter_context(nc.psum_tensor(f"p{i}", [128, TC], F32))
            for i in range(8))
        SEMS = {name: _ctx.enter_context(nc.semaphore("sm_" + name))
                for name in ALL_SEMS}
        block = _ctx.enter_context(nc.Block())
        P = [p0, p1, p2, p3, p4, p5, p6, p7]
        SEM = SEMS
        pdeps = [[] for _ in range(8)]

        def pacq(i):
            w = pdeps[i]
            pdeps[i] = []
            return w

        rb_sq = Rot([bsq[:, 0:TC], bsq[:, TC:2 * TC]])
        rb_w = Rot([wbufA[:, 0:1024], wbufA[:, 1024:2048]])
        rb_cb = Rot([cbuf[:, 0:TC + 6], cbuf[:, TC + 6:2 * (TC + 6)]])
        rb_exp = Rot([bexp[:, i * TC:(i + 1) * TC] for i in range(3)])
        rb_dbu = Rot([bdbu[:, i * TC:(i + 1) * TC] for i in range(4)])
        rb_bcB = Rot([bbcB[:, 0:TC], bbcB[:, TC:2 * TC]])
        rb_bcC = Rot([bbcC[:, 0:TC], bbcC[:, TC:2 * TC]])

        def ld(dst, src):
            return S.emit("sy", lambda e, d=dst, s=src: e.dma_start(out=d, in_=s))

        for dst, src in (
            (b_dtb[:], dt_b), (b_cvw[:], cv_w), (b_cvb[:], cv_b),
            (b_A[:], a_log), (b_dv[:], d_vec), (b_nw[:], nw),
            (b_ones1[:], ones1), (b_onesd[:], onesd), (b_id15[:], id15),
            (b_id128[:], id128), (b_rb[:], r_b), (b_sb1[:], sb1),
            (b_sb2h[:], sb2), (b_eb1[:], e_b1), (b_eb2h[:], e_b2),
            (b_wdt[:], w_dt), (b_cst[:], cst),
        ):
            ld(dst, src)
        for m in range(MI):
            ld(b_wxp[:, m * XD:(m + 1) * XD], w_xp[m * 128:(m + 1) * 128, :])
        for k in range(DC):
            ld(b_rw[:, k * NE:(k + 1) * NE], r_w[k * 128:(k + 1) * 128, :])
        WPL = [S.now(l) for l in DMA_LANES["sy"]]
        ta = S.emit("s", lambda e: e.activation(b_A[:], b_A[:], AF.Exp), waits=WPL)
        S.emit("v", lambda e: e.tensor_scalar_mul(b_A[:], b_A[:], -1.0), waits=[ta])
        S.emit("v", lambda e: e.tensor_scalar_mul(b_sb2h[:], b_sb2h[:], 0.5),
               waits=WPL)
        S.emit("v", lambda e: e.tensor_scalar_mul(b_eb2h[:], b_eb2h[:], 0.5),
               waits=WPL)
        S.emit("v", lambda e: e.memset(b_state[:], 0.0))
        S.emit("v", lambda e: e.memset(b_halo[:], 0.0))
        WPV = S.now("v")

        def emit_ln_stats(src, n_tc, pa, pb, pm, pr, src_ready):
            wA = pacq(pa) + src_ready + WPL
            tagm = None
            for k in range(DC):
                tagm = S.emit("t", lambda e, k=k: e.matmul(
                    P[pa][0:1, :], b_onesd[:], src[:, k * n_tc:(k + 1) * n_tc],
                    start=(k == 0), stop=(k == DC - 1)),
                    waits=(wA if k == 0 else ()))
            wB = pacq(pb)
            tagq = None
            for k in range(DC):
                ss, sap, sw = rb_sq.acquire()
                tsq = S.emit("s", lambda e, sap=sap, k=k: e.activation(
                    sap, src[:, k * n_tc:(k + 1) * n_tc], AF.Square),
                    waits=sw + src_ready)
                tagq = S.emit("t", lambda e, sap=sap, k=k: e.matmul(
                    P[pb][0:1, :], b_onesd[:], sap,
                    start=(k == 0), stop=(k == DC - 1)),
                    waits=[tsq] + (wB if k == 0 else []))
                rb_sq.used(ss, tagq)
            mrow = b_rows[0:1, 0:n_tc]
            vrow = b_rows[0:1, TC:TC + n_tc]
            t0 = S.emit("v", lambda e: e.tensor_copy(mrow, P[pa][0:1, :]),
                        waits=[tagm, tagq])
            t1 = S.emit("v", lambda e: e.tensor_mul(vrow, mrow, mrow),
                        waits=[t0])
            pdeps[pa] += [t0, t1]
            t2 = S.emit("v", lambda e: e.tensor_tensor(vrow, P[pb][0:1, :], vrow,
                                                       ALU.subtract),
                        waits=[t1])
            pdeps[pb].append(t2)
            t3 = S.emit("s", lambda e: e.activation(vrow, vrow, AF.Sqrt, bias=b_cst[0:1, 0:1]),
                        waits=[t2] + WPL)
            t4 = S.emit("v", lambda e: e.reciprocal(vrow, vrow), waits=[t3])
            tmb = S.emit("t", lambda e: e.matmul(P[pm][:], b_ones1[:], mrow,
                                                 start=True, stop=True),
                         waits=pacq(pm) + [t0] + WPL)
            trb = S.emit("t", lambda e: e.matmul(P[pr][:], b_ones1[:], vrow,
                                                 start=True, stop=True),
                         waits=pacq(pr) + [t4])
            return tmb, trb

        state_tags = {}

        for tau in range(NCH):
            S.barrier()
            c0 = tau * TC
            wx = []
            for k in range(DC):
                wx.append(S.emit("sy", lambda e, k=k, c0=c0: e.dma_start(
                    out=bx[:, k * TC:(k + 1) * TC],
                    in_=xT[k * 128:(k + 1) * 128, c0:c0 + TC])))
            tmb, trb = emit_ln_stats(bx, TC, 0, 1, 2, 3, wx)
            w_xn = None
            for k in range(DC):
                ts_ = S.emit("v", lambda e, k=k: e.tensor_tensor(
                    bx[:, k * TC:(k + 1) * TC], bx[:, k * TC:(k + 1) * TC],
                    P[2][:], ALU.subtract), waits=[tmb])
                w_xn = S.emit("v", lambda e, k=k: e.tensor_mul(
                    bx[:, k * TC:(k + 1) * TC], bx[:, k * TC:(k + 1) * TC],
                    P[3][:]), waits=[trb, ts_])
            pdeps[2].append(w_xn)
            pdeps[3].append(w_xn)
            w0 = pacq(0)
            tq = None
            for k in range(DC):
                ss, sap, sw = rb_sq.acquire()
                tsq = S.emit("s", lambda e, sap=sap, k=k: e.activation(
                    sap, bx[:, k * TC:(k + 1) * TC], AF.Square),
                    waits=sw + [w_xn])
                tq = S.emit("t", lambda e, sap=sap, k=k: e.matmul(
                    P[0][0:1, :], b_onesd[:], sap,
                    start=(k == 0), stop=(k == DC - 1)),
                    waits=[tsq] + (w0 if k == 0 else []))
                rb_sq.used(ss, tq)
            rrow = b_rows[0:1, 2 * TC:3 * TC]
            t3 = S.emit("s", lambda e: e.activation(rrow, P[0][0:1, :], AF.Sqrt,
                                                    bias=b_cst[0:1, 0:1]),
                        waits=[tq] + WPL)
            pdeps[0].append(t3)
            t4 = S.emit("v", lambda e: e.reciprocal(rrow, rrow), waits=[t3])
            trr = S.emit("t", lambda e: e.matmul(P[1][:], b_ones1[:], rrow,
                                                 start=True, stop=True),
                         waits=pacq(1) + [t4])
            w_h = None
            for k in range(DC):
                w_h = S.emit("v", lambda e, k=k: e.scalar_tensor_tensor(
                    bh[:, k * TC:(k + 1) * TC], bx[:, k * TC:(k + 1) * TC],
                    b_nw[:, k:k + 1], P[1][:], ALU.mult, ALU.mult),
                    waits=[trr, w_xn] + WPL)
            pdeps[1].append(w_h)

            for mg in range(2):
                chain_w = [pacq(i) for i in range(8)]
                tags_mm = [None] * 8
                for k in range(DC):
                    ws, wap, ww = rb_w.acquire()
                    tw = S.emit("sy", lambda e, wap=wap, k=k, mg=mg: e.dma_start(
                        out=wap, in_=w_in[k * 128:(k + 1) * 128,
                                         mg * 1024:(mg + 1) * 1024]),
                        waits=ww)
                    for mm in range(8):
                        tags_mm[mm] = S.emit("t", lambda e, wap=wap, mm=mm, k=k:
                                             e.matmul(
                            P[mm][:], wap[:, mm * 128:(mm + 1) * 128],
                            bh[:, k * TC:(k + 1) * TC],
                            start=(k == 0), stop=(k == DC - 1)),
                            waits=[tw, w_h] + (chain_w[mm] if k == 0 else []))
                        rb_w.used(ws, tags_mm[mm])
                if mg == 0:
                    tsil_u = [None] * 8
                    for mm in range(8):
                        cs, cap, cw = rb_cb.acquire()
                        th0 = S.emit("v", lambda e, cap=cap, mm=mm: e.tensor_copy(
                            cap[:, 0:3], b_halo[:, mm * 3:(mm + 1) * 3]),
                            waits=cw + [tags_mm[mm], WPV])
                        t1 = S.emit("v", lambda e, cap=cap, mm=mm: e.tensor_copy(
                            cap[:, 3:3 + TC], P[mm][:]), waits=[th0])
                        pdeps[mm].append(t1)
                        th2 = S.emit("v", lambda e, cap=cap, mm=mm: e.tensor_copy(
                            b_halo[:, mm * 3:(mm + 1) * 3], cap[:, TC:TC + 3]),
                            waits=[t1])
                        acc = bdu[:, mm * TC:(mm + 1) * TC]
                        tconv = S.emit("v", lambda e, cap=cap, acc=acc, mm=mm:
                               e.tensor_scalar(
                                   acc, cap[:, 0:TC],
                                   b_cvw[:, mm * 4:mm * 4 + 1], None, ALU.mult),
                               waits=WPL + [th2])
                        for kk in range(1, 4):
                            tconv = S.emit(
                                "v", lambda e, cap=cap, acc=acc, mm=mm, kk=kk:
                                e.scalar_tensor_tensor(
                                    acc, cap[:, kk:kk + TC],
                                    b_cvw[:, mm * 4 + kk:mm * 4 + kk + 1],
                                    acc, ALU.mult, ALU.add), waits=[tconv])
                        rb_cb.used(cs, tconv)
                        tsil_u[mm] = S.emit("s", lambda e, acc=acc, mm=mm:
                                            e.activation(
                            bu[:, mm * TC:(mm + 1) * TC], acc, AF.Silu,
                            bias=b_cvb[:, mm:mm + 1]), waits=[tconv] + WPL)
                else:
                    for mm in range(8):
                        t = S.emit("s", lambda e, mm=mm: e.activation(
                            bsr[:, mm * TC:(mm + 1) * TC], P[mm][:], AF.Silu),
                            waits=[tags_mm[mm]])
                        pdeps[mm].append(t)
            w_u = S.now("s")

            wxp = pacq(0)
            txd = None
            for m in range(MI):
                txd = S.emit("t", lambda e, m=m: e.matmul(
                    P[0][0:XD, :], b_wxp[:, m * XD:(m + 1) * XD],
                    bu[:, m * TC:(m + 1) * TC],
                    start=(m == 0), stop=(m == MI - 1)),
                    waits=[w_u] + WPL + (wxp if m == 0 else []))
            tev = S.emit("v", lambda e: e.tensor_copy(xdl[:], P[0][0:XD, :]),
                         waits=[txd])
            pdeps[0].append(tev)
            td1 = S.emit("g", lambda e, tau=tau: e.dma_start(out=xd_in[tau],
                                                             in_=xdl[:]),
                         waits=[tev])
            tcc = S.emit("g", lambda e, tau=tau: e.collective_compute(
                "AllGather", ALU.bypass, replica_groups=RG,
                ins=[xd_in[tau]], outs=[xd_out[tau]]),
                waits=[td1], sem="cc", n=1)
            td2 = S.emit("g", lambda e, tau=tau: e.dma_start(
                out=xdf[:], in_=xd_out[tau][0]), waits=[tcc])
            td3 = S.emit("g", lambda e, tau=tau: e.dma_start(
                out=xdf2[:], in_=xd_out[tau][1]), waits=[tcc])
            tsum = S.emit("v", lambda e: e.tensor_add(xdf[:], xdf[:], xdf2[:]),
                          waits=[td2, td3])
            td4 = S.emit("g", lambda e, tau=tau: e.dma_start(out=xdf_d[tau],
                                                             in_=xdf[:]),
                         waits=[tsum])

            w_dl = []
            for m in range(MI):
                pi = m % 2
                tdt = S.emit("t", lambda e, m=m, pi=pi: e.matmul(
                    P[pi][:], b_wdt[:, m * 128:(m + 1) * 128], xdf[0:DTR, :],
                    start=True, stop=True),
                    waits=[tsum] + WPL + pacq(pi))
                ss, sap, sw = rb_sq.acquire()
                te1 = S.emit("s", lambda e, sap=sap, m=m, pi=pi: e.activation(
                    sap, P[pi][:], AF.Exp, bias=b_dtb[:, m:m + 1]),
                    waits=[tdt] + WPL + sw)
                pdeps[pi].append(te1)
                tln = S.emit("s", lambda e, sap=sap, m=m: e.activation(
                    bdl[:, m * TC:(m + 1) * TC], sap, AF.Ln,
                    bias=b_cst[:, 1:2]), waits=WPL + [te1])
                rb_sq.used(ss, tln)
                tdu = S.emit("v", lambda e, m=m: e.tensor_mul(
                    bdu[:, m * TC:(m + 1) * TC], bdl[:, m * TC:(m + 1) * TC],
                    bu[:, m * TC:(m + 1) * TC]), waits=[tln])
                w_dl.append((tln, tdu))

            ty_tags = {}
            for n in range(NST):
                bs_, bB, wB = rb_bcB.acquire()
                tB = S.emit("sy", lambda e, bB=bB, n=n, tau=tau: e.dma_start(
                    out=bB, in_=xdf_d[tau][DTR + n:DTR + n + 1, :]
                    .to_broadcast((128, TC))), waits=wB + [td4])
                cs_, bC, wC = rb_bcC.acquire()
                tC = S.emit("sy", lambda e, bC=bC, n=n, tau=tau: e.dma_start(
                    out=bC, in_=xdf_d[tau][DTR + NST + n:DTR + NST + n + 1, :]
                    .to_broadcast((128, TC))), waits=wC + [td4])
                for m in range(MI):
                    tln, tdu = w_dl[m]
                    col = m * NST + n
                    es, eap, ew = rb_exp.acquire()
                    texp = S.emit("s", lambda e, eap=eap, m=m, col=col:
                                  e.activation(
                        eap, bdl[:, m * TC:(m + 1) * TC], AF.Exp,
                        scale=b_A[:, col:col + 1]),
                        waits=ew + [tln, WPV])
                    ds_, dap, dw = rb_dbu.acquire()
                    tdbu = S.emit("v", lambda e, dap=dap, bB=bB, m=m: e.tensor_mul(
                        dap, bdu[:, m * TC:(m + 1) * TC], bB),
                        waits=dw + [tdu, tB])
                    rb_bcB.used(bs_, tdbu)
                    tscan = S.emit("v", lambda e, eap=eap, dap=dap, col=col:
                                   e.tensor_tensor_scan(
                                       eap, eap, dap, b_state[:, col:col + 1],
                                       ALU.mult, ALU.add),
                                   waits=[texp, tdbu, state_tags.get(col)])
                    rb_dbu.used(ds_, tscan)
                    state_tags[col] = S.emit(
                        "v", lambda e, eap=eap, col=col: e.tensor_copy(
                            b_state[:, col:col + 1], eap[:, TC - 1:TC]),
                        waits=[tscan])
                    if n == 0:
                        ty = S.emit("v", lambda e, eap=eap, bC=bC, m=m:
                                    e.tensor_mul(
                            by[:, m * TC:(m + 1) * TC], eap, bC),
                            waits=[tC, tscan, ty_tags.get(m)])
                        rb_exp.used(es, ty)
                        rb_bcC.used(cs_, ty)
                    else:
                        ds2, dap2, dw2 = rb_dbu.acquire()
                        tsc = S.emit("v", lambda e, eap=eap, dap2=dap2, bC=bC:
                                     e.tensor_mul(dap2, eap, bC),
                                     waits=dw2 + [tC, tscan])
                        rb_exp.used(es, tsc)
                        rb_bcC.used(cs_, tsc)
                        ty = S.emit("v", lambda e, dap2=dap2, m=m: e.tensor_add(
                            by[:, m * TC:(m + 1) * TC],
                            by[:, m * TC:(m + 1) * TC], dap2),
                            waits=[tsc, ty_tags.get(m)])
                        rb_dbu.used(ds2, ty)
                    ty_tags[m] = ty

            w_y = [None] * MI
            for m in range(MI):
                typ = S.emit("v", lambda e, m=m: e.scalar_tensor_tensor(
                    by[:, m * TC:(m + 1) * TC], bu[:, m * TC:(m + 1) * TC],
                    b_dv[:, m:m + 1], by[:, m * TC:(m + 1) * TC],
                    ALU.mult, ALU.add), waits=WPL + [ty_tags.get(m)])
                w_y[m] = S.emit("v", lambda e, m=m: e.tensor_mul(
                    by[:, m * TC:(m + 1) * TC], by[:, m * TC:(m + 1) * TC],
                    bsr[:, m * TC:(m + 1) * TC]), waits=[typ])

            chain_w = [pacq(i) for i in range(8)]
            tmo = [None] * 8
            for m in range(MI):
                ws, wap, ww = rb_w.acquire()
                tw = S.emit("sy", lambda e, wap=wap, m=m: e.dma_start(
                    out=wap, in_=w_out[m * 128:(m + 1) * 128, :]), waits=ww)
                for mo in range(8):
                    tmo[mo] = S.emit("t", lambda e, wap=wap, mo=mo, m=m: e.matmul(
                        P[mo][:], wap[:, mo * 128:(mo + 1) * 128],
                        by[:, m * TC:(m + 1) * TC],
                        start=(m == 0), stop=(m == MI - 1)),
                        waits=[tw, w_y[m]] + (chain_w[mo] if m == 0 else []))
                    rb_w.used(ws, tmo[mo])
            w_pout = []
            for mo in range(8):
                t = S.emit("v", lambda e, mo=mo: e.scalar_tensor_tensor(
                    bh[:, mo * TC:(mo + 1) * TC], bx[:, mo * TC:(mo + 1) * TC],
                    0.5, P[mo][:], ALU.mult, ALU.add), waits=[tmo[mo]])
                pdeps[mo].append(t)
                w_pout.append(t)
            tpo = S.emit("g", lambda e, tau=tau: e.dma_start(
                out=p_in[tau].rearrange("(dc p) c -> p dc c", p=128),
                in_=bh[:].rearrange("p (dc c) -> p dc c", c=TC)),
                waits=w_pout)
            S.emit("g", lambda e, tau=tau: e.collective_compute(
                "AllGather", ALU.bypass, replica_groups=RG,
                ins=[p_in[tau]], outs=[p_out[tau]]),
                waits=[tpo], sem="cc", n=1)

        _mctx.close()
        bXn = sb("bXn", [128, DC * PT])
        racc = sb("racc", [128, DC * PT])
        att0 = sb("att0", [128, DC * TC])
        att1 = sb("att1", [128, DC * TC])
        bW1 = sb("bW1", [128, 8 * HF])
        bW2 = sb("bW2", [128, HC * D])
        bHb = sb("bHb", [128, HC * TC])
        cwbc = sb("cwbc", [128, 2 * TC])
        blg = sb("blg", [NE, TC])
        btl = sb("btl", [128, 2 * NE])
        bts = sb("bts", [128, 8 * NE])
        bcw = sb("bcw", [NE, PT])
        rb_cwb = Rot([cwbc[:, 0:TC], cwbc[:, TC:2 * TC]])

        for pp in range(NPASS):
            S.barrier()
            w_x1 = None
            for hh in range(2):
                cc_i = 2 * pp + hh
                wait_cc = ("cc", 2 * (cc_i + 1))
                ta0 = S.emit("sy", lambda e, cc_i=cc_i: e.dma_start(
                    out=att0[:].rearrange("p (dc c) -> p dc c", c=TC),
                    in_=p_out[cc_i][0].rearrange("(dc p) c -> p dc c", p=128)),
                    waits=[wait_cc, S.now("t"), S.now("v"), S.now("s")])
                ta1 = S.emit("sy", lambda e, cc_i=cc_i: e.dma_start(
                    out=att1[:].rearrange("p (dc c) -> p dc c", c=TC),
                    in_=p_out[cc_i][1].rearrange("(dc p) c -> p dc c", p=128)),
                    waits=[wait_cc, S.now("t"), S.now("v")])
                tadd = None
                for k in range(DC):
                    tadd = S.emit("v", lambda e, k=k: e.tensor_add(
                        att0[:, k * TC:(k + 1) * TC],
                        att0[:, k * TC:(k + 1) * TC],
                        att1[:, k * TC:(k + 1) * TC]), waits=[ta0, ta1])
                tmb, trb = emit_ln_stats(att0, TC, 0, 1, 2, 3, [tadd])
                wxf = []
                for k in range(DC):
                    wxf.append(S.emit("sy", lambda e, k=k, cc_i=cc_i: e.dma_start(
                        out=att1[:, k * TC:(k + 1) * TC],
                        in_=xT[k * 128:(k + 1) * 128,
                               cc_i * TC:(cc_i + 1) * TC]), waits=[tadd]))
                w_x1c = []
                for k in range(DC):
                    ts_ = S.emit("v", lambda e, k=k: e.tensor_tensor(
                        att0[:, k * TC:(k + 1) * TC],
                        att0[:, k * TC:(k + 1) * TC],
                        P[2][:], ALU.subtract), waits=[tmb])
                    tm_ = S.emit("v", lambda e, k=k: e.tensor_mul(
                        att0[:, k * TC:(k + 1) * TC],
                        att0[:, k * TC:(k + 1) * TC], P[3][:]),
                        waits=[trb, ts_])
                    t_ = S.emit("v", lambda e, k=k: e.tensor_add(
                        att0[:, k * TC:(k + 1) * TC],
                        att0[:, k * TC:(k + 1) * TC],
                        att1[:, k * TC:(k + 1) * TC]), waits=wxf + [tm_])
                    w_x1c.append(t_)
                pdeps[2].append(w_x1c[-1])
                pdeps[3].append(w_x1c[-1])
                tx1d = S.emit("g", lambda e, pp=pp, hh=hh: e.dma_start(
                    out=x1_d[pp].rearrange("(dc p) c -> p dc c", p=128)
                    [:, :, hh * TC:(hh + 1) * TC],
                    in_=att0[:].rearrange("p (dc c) -> p dc c", c=TC)),
                    waits=w_x1c)
                tmb, trb = emit_ln_stats(att0, TC, 0, 1, 2, 3, w_x1c)
                for k in range(DC):
                    ts_ = S.emit("v", lambda e, k=k: e.tensor_tensor(
                        att0[:, k * TC:(k + 1) * TC],
                        att0[:, k * TC:(k + 1) * TC],
                        P[2][:], ALU.subtract), waits=[tmb, tx1d])
                    w_x1 = S.emit("v", lambda e, k=k, hh=hh: e.tensor_mul(
                        bXn[:, k * PT + hh * TC:k * PT + (hh + 1) * TC],
                        att0[:, k * TC:(k + 1) * TC], P[3][:]),
                        waits=[trb, ts_])
                pdeps[2].append(w_x1)
                pdeps[3].append(w_x1)

            for tt in range(2):
                tlg = None
                wlg = pacq(4)
                for k in range(DC):
                    tlg = S.emit("t", lambda e, k=k, tt=tt: e.matmul(
                        P[4][0:NE, :], b_rw[:, k * NE:(k + 1) * NE],
                        bXn[:, k * PT + tt * TC:k * PT + (tt + 1) * TC],
                        start=(k == 0), stop=(k == DC - 1)),
                        waits=[w_x1] + WPL + (wlg if k == 0 else []))
                tb = S.emit("v", lambda e: e.tensor_scalar(
                    blg[:], P[4][0:NE, :], b_rb[:, 0:1], None, ALU.add),
                    waits=[tlg] + WPL)
                pdeps[4].append(tb)
                for g in range(NG):
                    pt_i = 5 if g % 2 == 0 else 6
                    ttr = S.emit("t", lambda e, g=g, pt_i=pt_i: e.transpose(
                        P[pt_i][:, 0:NE], blg[:, g * 128:(g + 1) * 128],
                        b_id15[:]), waits=[tb] + WPL + pacq(pt_i))
                    lt = btl[:, (g % 2) * NE:(g % 2 + 1) * NE]
                    tcp = S.emit("v", lambda e, lt=lt, pt_i=pt_i: e.tensor_copy(
                        lt, P[pt_i][:, 0:NE]), waits=[ttr])
                    pdeps[pt_i].append(tcp)
                    mx, sm_, m1, m2, m3 = (bts[:, i:i + 1] for i in range(5))
                    prb = bts[:, NE:2 * NE]
                    t1b = bts[:, 2 * NE:3 * NE]
                    t2b = bts[:, 3 * NE:4 * NE]
                    mskb = bts[:, 4 * NE:5 * NE]
                    S.emit("v", lambda e, lt=lt, mx=mx: e.reduce_max(
                        mx, lt, axis=AX.X), waits=[tcp])
                    S.emit("v", lambda e, lt=lt, mx=mx: e.tensor_scalar(
                        lt, lt, mx, None, ALU.subtract), waits=[S.now("v")])
                    te = S.emit("s", lambda e, lt=lt: e.activation(
                        lt, lt, AF.Exp), waits=[S.now("v")])
                    S.emit("v", lambda e, lt=lt, sm_=sm_: e.reduce_sum(
                        sm_, lt, axis=AX.X), waits=[te])
                    S.emit("v", lambda e, sm_=sm_: e.reciprocal(sm_, sm_),
                           waits=[S.now("v")])
                    S.emit("v", lambda e, lt=lt, sm_=sm_, prb=prb: e.tensor_scalar(
                        prb, lt, sm_, None, ALU.mult), waits=[S.now("v")])
                    S.emit("v", lambda e, prb=prb, m1=m1: e.reduce_max(
                        m1, prb, axis=AX.X), waits=[S.now("v")])
                    S.emit("v", lambda e, prb=prb, m1=m1, mskb=mskb:
                           e.tensor_scalar(mskb, prb, m1, None, ALU.is_lt),
                           waits=[S.now("v")])
                    S.emit("v", lambda e, prb=prb, mskb=mskb, t1b=t1b:
                           e.tensor_mul(t1b, prb, mskb), waits=[S.now("v")])
                    S.emit("v", lambda e, t1b=t1b, m2=m2: e.reduce_max(
                        m2, t1b, axis=AX.X), waits=[S.now("v")])
                    S.emit("v", lambda e, t1b=t1b, m2=m2, mskb=mskb:
                           e.tensor_scalar(mskb, t1b, m2, None, ALU.is_lt),
                           waits=[S.now("v")])
                    S.emit("v", lambda e, t1b=t1b, mskb=mskb, t2b=t2b:
                           e.tensor_mul(t2b, t1b, mskb), waits=[S.now("v")])
                    S.emit("v", lambda e, t2b=t2b, m3=m3: e.reduce_max(
                        m3, t2b, axis=AX.X), waits=[S.now("v")])
                    S.emit("v", lambda e, prb=prb, m3=m3, mskb=mskb:
                           e.tensor_scalar(mskb, prb, m3, None, ALU.is_ge),
                           waits=[S.now("v")])
                    tcw = S.emit("v", lambda e, lt=lt, prb=prb, mskb=mskb:
                                 e.tensor_mul(lt, prb, mskb),
                                 waits=[S.now("v")])
                    ttc = S.emit("t", lambda e, lt=lt: e.transpose(
                        P[7][0:NE, 0:128], lt, b_id128[:]),
                        waits=[tcw] + WPL + pacq(7))
                    tcc2 = S.emit("v", lambda e, g=g, tt=tt: e.tensor_copy(
                        bcw[:, tt * TC + g * 128:tt * TC + (g + 1) * 128],
                        P[7][0:NE, 0:128]), waits=[ttc])
                    pdeps[7].append(tcc2)
            w_cw = S.now("v")
            tcwd = S.emit("g", lambda e, pp=pp: e.dma_start(out=cw_d[pp],
                                                            in_=bcw[:]),
                          waits=[w_cw])

            racc_tags = {}
            for tt in range(2):
                for dc in range(DC):
                    pi = 2 + (dc % 2)
                    ti = S.emit("t", lambda e, dc=dc, tt=tt, pi=pi: e.matmul(
                        P[pi][:], b_eb2h[:, dc * 128:(dc + 1) * 128],
                        bcw[:, tt * TC:(tt + 1) * TC],
                        start=True, stop=True),
                        waits=[w_cw, WPV] + pacq(pi))
                    t = S.emit("v", lambda e, dc=dc, tt=tt, pi=pi: e.tensor_scalar(
                        racc[:, dc * PT + tt * TC:dc * PT + (tt + 1) * TC],
                        P[pi][:], b_sb2h[:, dc:dc + 1], None, ALU.add),
                        waits=[ti, WPV])
                    pdeps[pi].append(t)
                    racc_tags[(tt, dc)] = t

            for e_i in range(NE + 1):
                wg = [S.now("t")]
                if e_i < NE:
                    tw1 = S.emit("sy", lambda e, e_i=e_i: e.dma_start(
                        out=bW1[:].rearrange("p (k h) -> p k h", h=HF),
                        in_=e_w1[e_i].rearrange("(k p) h -> p k h", p=128)),
                        waits=wg)
                    tw2 = S.emit("sy", lambda e, e_i=e_i: e.dma_start(
                        out=bW2[:].rearrange("p (k d) -> p k d", d=D),
                        in_=e_w2[e_i].rearrange("(k p) d -> p k d", p=128)),
                        waits=wg)
                else:
                    tw1 = S.emit("sy", lambda e: e.dma_start(
                        out=bW1[:].rearrange("p (k h) -> p k h", h=HF),
                        in_=sw1.rearrange("(k p) h -> p k h", p=128)), waits=wg)
                    tw2 = S.emit("sy", lambda e: e.dma_start(
                        out=bW2[:].rearrange("p (k d) -> p k d", d=D),
                        in_=sw2.rearrange("(k p) d -> p k d", p=128)), waits=wg)
                for tt in range(2):
                    w_bh = []
                    for hc in range(HC):
                        pi = hc % 2
                        tH = None
                        wH = pacq(pi)
                        for k in range(DC):
                            tH = S.emit("t", lambda e, hc=hc, k=k, tt=tt, pi=pi:
                                        e.matmul(
                                P[pi][:],
                                bW1[:, k * HF + hc * 128:k * HF + (hc + 1) * 128],
                                bXn[:, k * PT + tt * TC:k * PT + (tt + 1) * TC],
                                start=(k == 0), stop=(k == DC - 1)),
                                waits=[tw1, w_x1] + (wH if k == 0 else []))
                        if e_i < NE:
                            bias = b_eb1[:, e_i * HC + hc:e_i * HC + hc + 1]
                            scr = att0[:, hc * TC:(hc + 1) * TC]
                            tr = S.emit("s", lambda e, scr=scr, pi=pi, bias=bias:
                                        e.activation(scr, P[pi][:], AF.Relu,
                                                     bias=bias),
                                        waits=[tH] + WPL)
                            pdeps[pi].append(tr)
                            cbs, cbap, cbw = rb_cwb.acquire()
                            tbc = S.emit("sy", lambda e, cbap=cbap, e_i=e_i,
                                         tt=tt, pp=pp: e.dma_start(
                                out=cbap,
                                in_=cw_d[pp][e_i:e_i + 1, tt * TC:(tt + 1) * TC]
                                .to_broadcast((128, TC))),
                                waits=cbw + [tcwd])
                            tm = S.emit("v", lambda e, scr=scr, cbap=cbap, hc=hc:
                                        e.tensor_mul(
                                bHb[:, hc * TC:(hc + 1) * TC], scr, cbap),
                                waits=[tr, tbc])
                            rb_cwb.used(cbs, tm)
                            w_bh.append(tm)
                        else:
                            bias = b_sb1[:, hc:hc + 1]
                            tr = S.emit("s", lambda e, hc=hc, pi=pi, bias=bias:
                                        e.activation(
                                bHb[:, hc * TC:(hc + 1) * TC], P[pi][:],
                                AF.Relu, bias=bias), waits=[tH] + WPL)
                            pdeps[pi].append(tr)
                            w_bh.append(tr)
                    for dc in range(DC):
                        pi = 2 + (dc % 2)
                        tR = None
                        wR = pacq(pi)
                        for hk in range(HC):
                            tR = S.emit("t", lambda e, dc=dc, hk=hk, pi=pi:
                                        e.matmul(
                                P[pi][:],
                                bW2[:, hk * D + dc * 128:hk * D + (dc + 1) * 128],
                                bHb[:, hk * TC:(hk + 1) * TC],
                                start=(hk == 0), stop=(hk == HC - 1)),
                                waits=[tw2] + w_bh + (wR if hk == 0 else []))
                        t = S.emit("v", lambda e, dc=dc, tt=tt, pi=pi:
                                   e.scalar_tensor_tensor(
                            racc[:, dc * PT + tt * TC:dc * PT + (tt + 1) * TC],
                            P[pi][:], 1.0,
                            racc[:, dc * PT + tt * TC:dc * PT + (tt + 1) * TC],
                            ALU.mult, ALU.add),
                            waits=[tR, racc_tags.get((tt, dc))])
                        pdeps[pi].append(t)
                        racc_tags[(tt, dc)] = t
            w_racc = S.now("v")

            trt = S.emit("g", lambda e, pp=pp: e.dma_start(
                out=rt_in[pp].rearrange("(dc p) c -> p dc c", p=128),
                in_=racc[:].rearrange("p (dc c) -> p dc c", c=PT)),
                waits=[w_racc])
            trc = S.emit("g", lambda e, pp=pp: e.collective_compute(
                "AllGather", ALU.bypass, replica_groups=RG,
                ins=[rt_in[pp]], outs=[rt_out[pp]]),
                waits=[trt], sem="cc", n=1)

            tout = None
            for hh in range(2):
                tf0 = S.emit("sy", lambda e, pp=pp, hh=hh: e.dma_start(
                    out=att0[:].rearrange("p (dc c) -> p dc c", c=TC),
                    in_=rt_out[pp][0].rearrange("(dc p) c -> p dc c", p=128)
                    [:, :, hh * TC:(hh + 1) * TC]),
                    waits=[trc, S.now("s"), S.now("t"), S.now("v"), tout])
                tf1 = S.emit("sy", lambda e, pp=pp, hh=hh: e.dma_start(
                    out=att1[:].rearrange("p (dc c) -> p dc c", c=TC),
                    in_=rt_out[pp][1].rearrange("(dc p) c -> p dc c", p=128)
                    [:, :, hh * TC:(hh + 1) * TC]),
                    waits=[trc, S.now("t"), S.now("v"), tout])
                tadd = None
                for k in range(DC):
                    tadd = S.emit("v", lambda e, k=k: e.tensor_add(
                        att0[:, k * TC:(k + 1) * TC],
                        att0[:, k * TC:(k + 1) * TC],
                        att1[:, k * TC:(k + 1) * TC]), waits=[tf0, tf1])
                tmb, trb = emit_ln_stats(att0, TC, 0, 1, 2, 3, [tadd])
                tx1l = S.emit("sy", lambda e, pp=pp, hh=hh: e.dma_start(
                    out=att1[:].rearrange("p (dc c) -> p dc c", c=TC),
                    in_=x1_d[pp].rearrange("(dc p) c -> p dc c", p=128)
                    [:, :, hh * TC:(hh + 1) * TC]), waits=[tadd])
                w_fin = []
                for k in range(DC):
                    ts_ = S.emit("v", lambda e, k=k: e.tensor_tensor(
                        att0[:, k * TC:(k + 1) * TC],
                        att0[:, k * TC:(k + 1) * TC],
                        P[2][:], ALU.subtract), waits=[tmb])
                    tm_ = S.emit("v", lambda e, k=k: e.tensor_mul(
                        att0[:, k * TC:(k + 1) * TC],
                        att0[:, k * TC:(k + 1) * TC], P[3][:]),
                        waits=[trb, ts_])
                    t = S.emit("v", lambda e, k=k: e.tensor_add(
                        att1[:, k * TC:(k + 1) * TC],
                        att0[:, k * TC:(k + 1) * TC],
                        att1[:, k * TC:(k + 1) * TC]),
                        waits=[tm_, tx1l])
                    w_fin.append(t)
                pdeps[2].append(w_fin[-1])
                pdeps[3].append(w_fin[-1])
                tout = S.emit("g", lambda e, pp=pp, hh=hh: e.dma_start(
                    out=out.rearrange("(dc p) c -> p dc c", p=128)
                    [:, :, (2 * pp + hh) * TC:(2 * pp + hh + 1) * TC],
                    in_=att1[:].rearrange("p (dc c) -> p dc c", c=TC)),
                    waits=w_fin)

        S.emit("g", lambda e: e.nop(),
               waits=[S.now("dg0"), S.now("dg1")], sem=None)

        def replay(h, name):
            for rw, fn, sem, n in S.ops[name]:
                for sname, v in rw:
                    h.wait_ge(SEM[sname], v)
                r = fn(h)
                if sem is not None:
                    r.then_inc(SEM[sem], n)

        @block.sync
        def _(sync):
            replay(sync, "sy")

        @block.tensor
        def _(tensor):
            replay(tensor, "t")

        @block.vector
        def _(vector):
            replay(vector, "v")

        @block.scalar
        def _(scalar):
            replay(scalar, "s")

        @block.gpsimd
        def _(gpsimd):
            replay(gpsimd, "g")

    return nc


def _prep_core_inputs(inp, b, j):
    f32 = np.float32
    x = np.asarray(inp["L0_attn_x"], dtype=f32)
    xT = np.ascontiguousarray(x[b].T)

    w_inT = np.asarray(inp["L0_attn_in_proj_w"], dtype=f32).T
    w_in = np.ascontiguousarray(np.concatenate(
        [w_inT[:, j * DIH:(j + 1) * DIH],
         w_inT[:, DI + j * DIH:DI + (j + 1) * DIH]], axis=1))

    w_xp = np.ascontiguousarray(
        np.asarray(inp["L0_attn_x_proj_w"], dtype=f32).T[j * DIH:(j + 1) * DIH])
    w_dt = np.ascontiguousarray(
        np.asarray(inp["L0_attn_dt_proj_w"], dtype=f32).T[:, j * DIH:(j + 1) * DIH])

    def pack_pp(vec, nt):
        return np.ascontiguousarray(vec.reshape(nt, 128).T)

    dt_b = pack_pp(np.asarray(inp["L0_attn_dt_proj_b"], dtype=f32)
                   [j * DIH:(j + 1) * DIH], MI)
    cvw = np.asarray(inp["L0_attn_conv_w"], dtype=f32)[:, 0, :][j * DIH:(j + 1) * DIH]
    cv_w = np.ascontiguousarray(
        cvw.reshape(MI, 128, 4).transpose(1, 0, 2).reshape(128, MI * 4))
    cv_b = pack_pp(np.asarray(inp["L0_attn_conv_b"], dtype=f32)
                   [j * DIH:(j + 1) * DIH], MI)
    al = np.asarray(inp["L0_attn_A_log"], dtype=f32)[j * DIH:(j + 1) * DIH]
    a_log = np.ascontiguousarray(
        al.reshape(MI, 128, NST).transpose(1, 0, 2).reshape(128, MI * NST))
    d_vec = pack_pp(np.asarray(inp["L0_attn_D"], dtype=f32)[j * DIH:(j + 1) * DIH], MI)
    nw = pack_pp(np.asarray(inp["L0_attn_norm_w"], dtype=f32), DC)
    w_out = np.ascontiguousarray(
        np.asarray(inp["L0_attn_out_proj_w"], dtype=f32).T[j * DIH:(j + 1) * DIH])

    r_w = np.ascontiguousarray(np.asarray(inp["L0_ffn_router_W"], dtype=f32))
    r_b = np.ascontiguousarray(
        np.asarray(inp["L0_ffn_router_b"], dtype=f32).reshape(NE, 1))
    sw1 = np.ascontiguousarray(
        np.asarray(inp["L0_ffn_sW1"], dtype=f32)[:, j * HF:(j + 1) * HF])
    sb1 = pack_pp(np.asarray(inp["L0_ffn_sb1"], dtype=f32)[j * HF:(j + 1) * HF], HC)
    sw2 = np.ascontiguousarray(
        np.asarray(inp["L0_ffn_sW2"], dtype=f32)[j * HF:(j + 1) * HF])
    sb2 = pack_pp(np.asarray(inp["L0_ffn_sb2"], dtype=f32), DC)
    e_w1 = np.ascontiguousarray(
        np.asarray(inp["L0_ffn_eW1"], dtype=f32)[:, :, j * HF:(j + 1) * HF])
    eb1 = np.asarray(inp["L0_ffn_eb1"], dtype=f32)[:, j * HF:(j + 1) * HF]
    e_b1 = np.ascontiguousarray(
        eb1.reshape(NE, HC, 128).transpose(2, 0, 1).reshape(128, NE * HC))
    e_w2 = np.ascontiguousarray(
        np.asarray(inp["L0_ffn_eW2"], dtype=f32)[:, j * HF:(j + 1) * HF])
    e_b2 = np.ascontiguousarray(np.asarray(inp["L0_ffn_eb2"], dtype=f32))

    return dict(
        xT=xT, w_in=w_in, w_xp=w_xp, w_dt=w_dt, dt_b=dt_b, cv_w=cv_w, cv_b=cv_b,
        a_log=a_log, d_vec=d_vec, nw=nw, w_out=w_out, r_w=r_w, r_b=r_b,
        sw1=sw1, sb1=sb1, sw2=sw2, sb2=sb2, e_w1=e_w1, e_b1=e_b1, e_w2=e_w2,
        e_b2=e_b2,
        ones1=np.ones((1, 128), dtype=f32),
        onesd=np.full((128, 1), 1.0 / D, dtype=f32),
        id15=np.eye(NE, dtype=f32),
        id128=np.eye(128, dtype=f32),
        cst=np.stack([np.full(128, EPS, dtype=f32),
                      np.ones(128, dtype=f32)], axis=1),
    )


def kernel(**inp):
    TOK = np.asarray(inp["L0_attn_x"]).shape[1]
    nc = build_nc(TOK=TOK, TC=512)
    in_maps = [_prep_core_inputs(inp, c // 2, c % 2) for c in range(8)]
    res = run_bass_kernel_spmd(nc, in_maps, core_ids=list(range(8)))
    outs = [np.asarray(res.results[2 * b]["out"]).T for b in range(4)]
    return np.stack(outs).astype(np.float32)


# version 03
# speedup vs baseline: 1.0438x; 1.0438x over previous
import numpy as np
import concourse.bass as bass
import concourse.mybir as mybir
from concourse.bass_utils import run_bass_kernel_spmd

F32 = mybir.dt.float32
AF = mybir.ActivationFunctionType
ALU = mybir.AluOpType
AX = mybir.AxisListType

D = 1024
DI = 2048
DIH = 1024
NST = 16
DTR = 64
NE = 15
HF = 512
DC = 8
MI = 8
HC = HF // 128
XD = DTR + 2 * NST
EPS = 1e-5

ENGS = ("sy", "t", "v", "s", "g")
DMA_LANES = {"sy": ("ds0", "ds1", "ds2", "ds3"), "g": ("dg0", "dg1")}
ALL_SEMS = ("ds0", "ds1", "ds2", "ds3", "dg0", "dg1", "t", "v", "s", "cc")


class Sched:

    def __init__(self):
        self.ops = {e: [] for e in ENGS}
        self.cnt = {k: 0 for k in ALL_SEMS}
        self.lastw = {}
        self.rr = {"sy": 0, "g": 0}

    def emit(self, eng, fn, waits=(), sem="default", n=None):
        waits = [w for w in waits if w is not None]
        if sem == "default":
            if eng in DMA_LANES:
                lanes = DMA_LANES[eng]
                sem = lanes[self.rr[eng] % len(lanes)]
                self.rr[eng] += 1
                n = 16
                waits = waits + [(sem, self.cnt[sem])]
            else:
                sem, n = {"t": ("t", 1), "v": ("v", 1), "s": ("s", 1)}[eng]
        elif sem == "cc":
            waits = waits + [("cc", self.cnt["cc"])]
        best = {}
        for w in waits:
            if w is None:
                continue
            s, v = w
            assert v <= self.cnt[s], f"wait on future {s}:{v} > {self.cnt[s]}"
            if best.get(s, 0) < v:
                best[s] = v
        rw = []
        for s, v in best.items():
            key = (eng, s)
            if self.lastw.get(key, 0) < v:
                self.lastw[key] = v
                rw.append((s, v))
        tag = None
        if sem is not None:
            self.cnt[sem] += n
            tag = (sem, self.cnt[sem])
        self.ops[eng].append((tuple(rw), fn, sem, n))
        return tag

    def now(self, sem):
        return (sem, self.cnt[sem])

    def barrier(self):
        snap = [(s, self.cnt[s]) for s in self.cnt if self.cnt[s] > 0]
        for e in ENGS:
            self.emit(e, lambda h: h.nop(), waits=snap, sem=None)


class Rot:

    def __init__(self, aps):
        self.aps = list(aps)
        self.deps = [[] for _ in self.aps]
        self.i = -1

    def acquire(self):
        self.i += 1
        s = self.i % len(self.aps)
        w = self.deps[s]
        self.deps[s] = []
        return s, self.aps[s], w

    def used(self, s, tag):
        if tag is not None:
            self.deps[s].append(tag)


def build_nc(TOK=4096, TC=512):
    NCH = TOK // TC
    NPASS = NCH // 2
    PT = 2 * TC
    NG = TC // 128

    nc = bass.Bass("TRN2", target_bir_lowering=False)

    def di(name, shape):
        return nc.dram_tensor(name, shape, F32, kind="ExternalInput").ap()

    xT = di("xT", [D, TOK])
    w_in = di("w_in", [D, DI])
    w_xp = di("w_xp", [DIH, XD])
    w_dt = di("w_dt", [DTR, DIH])
    dt_b = di("dt_b", [128, MI])
    cv_w = di("cv_w", [128, MI * 4])
    cv_b = di("cv_b", [128, MI])
    a_log = di("a_log", [128, MI * NST])
    d_vec = di("d_vec", [128, MI])
    nw = di("nw", [128, DC])
    w_out = di("w_out", [DIH, D])
    r_w = di("r_w", [D, NE])
    r_b = di("r_b", [NE, 1])
    sw1 = di("sw1", [D, HF])
    sb1 = di("sb1", [128, HC])
    sw2 = di("sw2", [HF, D])
    sb2 = di("sb2", [128, DC])
    e_w1 = di("e_w1", [NE, D, HF])
    e_b1 = di("e_b1", [128, NE * HC])
    e_w2 = di("e_w2", [NE, HF, D])
    e_b2 = di("e_b2", [NE, D])
    ones1 = di("ones1", [1, 128])
    onesd = di("onesd", [128, 1])
    id15 = di("id15", [NE, NE])
    id128 = di("id128", [128, 128])
    cst = di("cst", [128, 2])

    out = nc.dram_tensor("out", [D, TOK], F32, kind="ExternalOutput").ap()

    xd_in = nc.dram_tensor("xd_in", [NCH, XD, TC], F32).ap()
    xd_out = nc.dram_tensor("xd_out", [NCH, 2, XD, TC], F32).ap()
    xdf_d = nc.dram_tensor("xdf_d", [NCH, XD, TC], F32).ap()
    p_in = nc.dram_tensor("p_in", [NCH, D, TC], F32).ap()
    p_out = nc.dram_tensor("p_out", [NCH, 2, D, TC], F32).ap()
    cw_d = nc.dram_tensor("cw_d", [NPASS, NE, PT], F32).ap()
    rt_in = nc.dram_tensor("rt_in", [NPASS, D, PT], F32).ap()
    rt_out = nc.dram_tensor("rt_out", [NPASS, 2, D, PT], F32).ap()
    x1_d = nc.dram_tensor("x1_d", [NPASS, D, PT], F32).ap()

    S = Sched()
    RG = [[0, 1], [2, 3], [4, 5], [6, 7]]

    from contextlib import ExitStack
    with ExitStack() as _ctx:
        def sb(name, shape):
            return _ctx.enter_context(nc.sbuf_tensor(name, shape, F32))

        b_dtb = sb("b_dtb", [128, MI])
        b_cvw = sb("b_cvw", [128, MI * 4])
        b_cvb = sb("b_cvb", [128, MI])
        b_A = sb("b_A", [128, MI * NST])
        b_dv = sb("b_dv", [128, MI])
        b_nw = sb("b_nw", [128, DC])
        b_ones1 = sb("b_ones1", [1, 128])
        b_onesd = sb("b_onesd", [128, 1])
        b_id15 = sb("b_id15", [NE, NE])
        b_id128 = sb("b_id128", [128, 128])
        b_rb = sb("b_rb", [NE, 1])
        b_sb1 = sb("b_sb1", [128, HC])
        b_sb2h = sb("b_sb2h", [128, DC])
        b_eb1 = sb("b_eb1", [128, NE * HC])
        b_eb2h = sb("b_eb2h", [NE, D])
        b_wxp = sb("b_wxp", [128, MI * XD])
        b_wdt = sb("b_wdt", [DTR, DIH])
        b_rw = sb("b_rw", [128, DC * NE])
        b_state = sb("b_state", [128, MI * NST])
        b_halo = sb("b_halo", [128, MI * 3])
        b_rows = sb("b_rows", [1, 4 * TC])
        b_cst = sb("b_cst", [128, 2])
        bsq = sb("bsq", [128, 2 * TC])
        _mctx = ExitStack()
        def sbm(name, shape):
            return _mctx.enter_context(nc.sbuf_tensor(name, shape, F32))
        bx = sbm("bx", [128, DC * TC])
        bh = sbm("bh", [128, DC * TC])
        bu = sbm("bu", [128, MI * TC])
        bsr = sbm("bsr", [128, MI * TC])
        bdl = sbm("bdl", [128, MI * TC])
        bdu = sbm("bdu", [128, MI * TC])
        by = sbm("by", [128, MI * TC])
        bexp = sbm("bexp", [128, 3 * TC])
        bdbu = sbm("bdbu", [128, 4 * TC])
        bbcB = sbm("bbcB", [128, 2 * TC])
        bbcC = sbm("bbcC", [128, 2 * TC])
        cbuf = sbm("cbuf", [128, 2 * (TC + 6)])
        wbufA = sbm("wbufA", [128, 4 * 512])
        xdl = sbm("xdl", [XD, TC])
        xdf = sbm("xdf", [XD, TC])
        xdf2 = sbm("xdf2", [XD, TC])
        p0, p1, p2, p3, p4, p5, p6, p7 = (
            _ctx.enter_context(nc.psum_tensor(f"p{i}", [128, TC], F32))
            for i in range(8))
        SEMS = {name: _ctx.enter_context(nc.semaphore("sm_" + name))
                for name in ALL_SEMS}
        block = _ctx.enter_context(nc.Block())
        P = [p0, p1, p2, p3, p4, p5, p6, p7]
        SEM = SEMS
        pdeps = [[] for _ in range(8)]

        def pacq(i):
            w = pdeps[i]
            pdeps[i] = []
            return w

        rb_sq = Rot([bsq[:, 0:TC], bsq[:, TC:2 * TC]])
        rb_w = Rot([wbufA[:, i * 512:(i + 1) * 512] for i in range(4)])
        rb_cb = Rot([cbuf[:, 0:TC + 6], cbuf[:, TC + 6:2 * (TC + 6)]])
        rb_exp = Rot([bexp[:, i * TC:(i + 1) * TC] for i in range(3)])
        rb_dbu = Rot([bdbu[:, i * TC:(i + 1) * TC] for i in range(4)])
        rb_bcB = Rot([bbcB[:, 0:TC], bbcB[:, TC:2 * TC]])
        rb_bcC = Rot([bbcC[:, 0:TC], bbcC[:, TC:2 * TC]])

        def ld(dst, src):
            return S.emit("sy", lambda e, d=dst, s=src: e.dma_start(out=d, in_=s))

        for dst, src in (
            (b_dtb[:], dt_b), (b_cvw[:], cv_w), (b_cvb[:], cv_b),
            (b_A[:], a_log), (b_dv[:], d_vec), (b_nw[:], nw),
            (b_ones1[:], ones1), (b_onesd[:], onesd), (b_id15[:], id15),
            (b_id128[:], id128), (b_rb[:], r_b), (b_sb1[:], sb1),
            (b_sb2h[:], sb2), (b_eb1[:], e_b1), (b_eb2h[:], e_b2),
            (b_wdt[:], w_dt), (b_cst[:], cst),
        ):
            ld(dst, src)
        for m in range(MI):
            ld(b_wxp[:, m * XD:(m + 1) * XD], w_xp[m * 128:(m + 1) * 128, :])
        for k in range(DC):
            ld(b_rw[:, k * NE:(k + 1) * NE], r_w[k * 128:(k + 1) * 128, :])
        WPL = [S.now(l) for l in DMA_LANES["sy"]]
        ta = S.emit("s", lambda e: e.activation(b_A[:], b_A[:], AF.Exp), waits=WPL)
        S.emit("v", lambda e: e.tensor_scalar_mul(b_A[:], b_A[:], -1.0), waits=[ta])
        S.emit("v", lambda e: e.tensor_scalar_mul(b_sb2h[:], b_sb2h[:], 0.5),
               waits=WPL)
        S.emit("v", lambda e: e.tensor_scalar_mul(b_eb2h[:], b_eb2h[:], 0.5),
               waits=WPL)
        S.emit("v", lambda e: e.memset(b_state[:], 0.0))
        S.emit("v", lambda e: e.memset(b_halo[:], 0.0))
        WPV = S.now("v")

        rows_dep = []

        def emit_ln_stats(src, n_tc, pa, pb, pm, pr, src_ready):
            wA = pacq(pa) + src_ready + WPL
            tagm = None
            for k in range(DC):
                tagm = S.emit("t", lambda e, k=k: e.matmul(
                    P[pa][0:1, :], b_onesd[:], src[:, k * n_tc:(k + 1) * n_tc],
                    start=(k == 0), stop=(k == DC - 1)),
                    waits=(wA if k == 0 else ()))
            wB = pacq(pb)
            tagq = None
            for k in range(DC):
                ss, sap, sw = rb_sq.acquire()
                tsq = S.emit("s", lambda e, sap=sap, k=k: e.activation(
                    sap, src[:, k * n_tc:(k + 1) * n_tc], AF.Square),
                    waits=sw + src_ready)
                tagq = S.emit("t", lambda e, sap=sap, k=k: e.matmul(
                    P[pb][0:1, :], b_onesd[:], sap,
                    start=(k == 0), stop=(k == DC - 1)),
                    waits=[tsq] + (wB if k == 0 else []))
                rb_sq.used(ss, tagq)
            mrow = b_rows[0:1, 0:n_tc]
            vrow = b_rows[0:1, TC:TC + n_tc]
            t0 = S.emit("v", lambda e: e.tensor_copy(mrow, P[pa][0:1, :]),
                        waits=[tagm, tagq] + rows_dep)
            t1 = S.emit("v", lambda e: e.tensor_mul(vrow, mrow, mrow),
                        waits=[t0])
            pdeps[pa] += [t0, t1]
            t2 = S.emit("v", lambda e: e.tensor_tensor(vrow, P[pb][0:1, :], vrow,
                                                       ALU.subtract),
                        waits=[t1])
            pdeps[pb].append(t2)
            t3 = S.emit("s", lambda e: e.activation(vrow, vrow, AF.Sqrt, bias=b_cst[0:1, 0:1]),
                        waits=[t2] + WPL)
            t4 = S.emit("v", lambda e: e.reciprocal(vrow, vrow), waits=[t3])
            tmb = S.emit("t", lambda e: e.matmul(P[pm][:], b_ones1[:], mrow,
                                                 start=True, stop=True),
                         waits=pacq(pm) + [t0] + WPL)
            trb = S.emit("t", lambda e: e.matmul(P[pr][:], b_ones1[:], vrow,
                                                 start=True, stop=True),
                         waits=pacq(pr) + [t4])
            rows_dep[:] = [tmb, trb]
            return tmb, trb

        state_tags = {}
        ty_tags = {}
        xdep = {"bx": [], "bh": [], "bu": [], "bsr": [], "bdl": [], "by": [],
                "xdl": [], "xdf": []}

        for tau in range(NCH):
            c0 = tau * TC
            wx = []
            for k in range(DC):
                wx.append(S.emit("sy", lambda e, k=k, c0=c0: e.dma_start(
                    out=bx[:, k * TC:(k + 1) * TC],
                    in_=xT[k * 128:(k + 1) * 128, c0:c0 + TC]),
                    waits=xdep["bx"]))
            tmb, trb = emit_ln_stats(bx, TC, 0, 1, 2, 3, wx)
            w_xn = None
            for k in range(DC):
                ts_ = S.emit("v", lambda e, k=k: e.tensor_tensor(
                    bx[:, k * TC:(k + 1) * TC], bx[:, k * TC:(k + 1) * TC],
                    P[2][:], ALU.subtract), waits=[tmb])
                w_xn = S.emit("v", lambda e, k=k: e.tensor_mul(
                    bx[:, k * TC:(k + 1) * TC], bx[:, k * TC:(k + 1) * TC],
                    P[3][:]), waits=[trb, ts_])
            pdeps[2].append(w_xn)
            pdeps[3].append(w_xn)
            w0 = pacq(0)
            tq = None
            for k in range(DC):
                ss, sap, sw = rb_sq.acquire()
                tsq = S.emit("s", lambda e, sap=sap, k=k: e.activation(
                    sap, bx[:, k * TC:(k + 1) * TC], AF.Square),
                    waits=sw + [w_xn])
                tq = S.emit("t", lambda e, sap=sap, k=k: e.matmul(
                    P[0][0:1, :], b_onesd[:], sap,
                    start=(k == 0), stop=(k == DC - 1)),
                    waits=[tsq] + (w0 if k == 0 else []))
                rb_sq.used(ss, tq)
            rrow = b_rows[0:1, 2 * TC:3 * TC]
            t3 = S.emit("s", lambda e: e.activation(rrow, P[0][0:1, :], AF.Sqrt,
                                                    bias=b_cst[0:1, 0:1]),
                        waits=[tq] + WPL)
            pdeps[0].append(t3)
            t4 = S.emit("v", lambda e: e.reciprocal(rrow, rrow), waits=[t3])
            trr = S.emit("t", lambda e: e.matmul(P[1][:], b_ones1[:], rrow,
                                                 start=True, stop=True),
                         waits=pacq(1) + [t4])
            w_h = None
            for k in range(DC):
                w_h = S.emit("v", lambda e, k=k: e.scalar_tensor_tensor(
                    bh[:, k * TC:(k + 1) * TC], bx[:, k * TC:(k + 1) * TC],
                    b_nw[:, k:k + 1], P[1][:], ALU.mult, ALU.mult),
                    waits=[trr, w_xn] + WPL + xdep["bh"])
            pdeps[1].append(w_h)

            tsil_u = [None] * 8
            t_inproj_last = None
            for mg4 in range(4):
                chain_w = [pacq(q) for q in range(4)]
                tags4 = [None] * 4
                for k in range(DC):
                    ws, wap, ww = rb_w.acquire()
                    tw = S.emit("sy", lambda e, wap=wap, k=k, mg4=mg4: e.dma_start(
                        out=wap, in_=w_in[k * 128:(k + 1) * 128,
                                         mg4 * 512:(mg4 + 1) * 512]),
                        waits=ww)
                    for q in range(4):
                        tags4[q] = S.emit("t", lambda e, wap=wap, q=q, k=k:
                                          e.matmul(
                            P[q][:], wap[:, q * 128:(q + 1) * 128],
                            bh[:, k * TC:(k + 1) * TC],
                            start=(k == 0), stop=(k == DC - 1)),
                            waits=[tw, w_h] + (chain_w[q] if k == 0 else []))
                        rb_w.used(ws, tags4[q])
                t_inproj_last = tags4[3]
                if mg4 < 2:
                    for q in range(4):
                        mm = mg4 * 4 + q
                        cs, cap, cw = rb_cb.acquire()
                        th0 = S.emit("v", lambda e, cap=cap, mm=mm: e.tensor_copy(
                            cap[:, 0:3], b_halo[:, mm * 3:(mm + 1) * 3]),
                            waits=cw + [tags4[q], WPV])
                        t1 = S.emit("v", lambda e, cap=cap, q=q: e.tensor_copy(
                            cap[:, 3:3 + TC], P[q][:]), waits=[th0])
                        pdeps[q].append(t1)
                        th2 = S.emit("v", lambda e, cap=cap, mm=mm: e.tensor_copy(
                            b_halo[:, mm * 3:(mm + 1) * 3], cap[:, TC:TC + 3]),
                            waits=[t1])
                        acc = bdu[:, mm * TC:(mm + 1) * TC]
                        tconv = S.emit("v", lambda e, cap=cap, acc=acc, mm=mm:
                               e.tensor_scalar(
                                   acc, cap[:, 0:TC],
                                   b_cvw[:, mm * 4:mm * 4 + 1], None, ALU.mult),
                               waits=WPL + [th2])
                        for kk in range(1, 4):
                            tconv = S.emit(
                                "v", lambda e, cap=cap, acc=acc, mm=mm, kk=kk:
                                e.scalar_tensor_tensor(
                                    acc, cap[:, kk:kk + TC],
                                    b_cvw[:, mm * 4 + kk:mm * 4 + kk + 1],
                                    acc, ALU.mult, ALU.add), waits=[tconv])
                        rb_cb.used(cs, tconv)
                        tsil_u[mm] = S.emit("s", lambda e, acc=acc, mm=mm:
                                            e.activation(
                            bu[:, mm * TC:(mm + 1) * TC], acc, AF.Silu,
                            bias=b_cvb[:, mm:mm + 1]),
                            waits=[tconv] + WPL + xdep["bu"])
                else:
                    for q in range(4):
                        mm = (mg4 - 2) * 4 + q
                        t = S.emit("s", lambda e, mm=mm, q=q: e.activation(
                            bsr[:, mm * TC:(mm + 1) * TC], P[q][:], AF.Silu),
                            waits=[tags4[q]] + xdep["bsr"])
                        pdeps[q].append(t)
            w_u = S.now("s")

            wxp = pacq(4)
            txd = None
            for m in range(MI):
                txd = S.emit("t", lambda e, m=m: e.matmul(
                    P[4][0:XD, :], b_wxp[:, m * XD:(m + 1) * XD],
                    bu[:, m * TC:(m + 1) * TC],
                    start=(m == 0), stop=(m == MI - 1)),
                    waits=[w_u] + WPL + (wxp if m == 0 else []))
            tev = S.emit("v", lambda e: e.tensor_copy(xdl[:], P[4][0:XD, :]),
                         waits=[txd] + xdep["xdl"])
            pdeps[4].append(tev)
            td1 = S.emit("g", lambda e, tau=tau: e.dma_start(out=xd_in[tau],
                                                             in_=xdl[:]),
                         waits=[tev])
            tcc = S.emit("g", lambda e, tau=tau: e.collective_compute(
                "AllGather", ALU.bypass, replica_groups=RG,
                ins=[xd_in[tau]], outs=[xd_out[tau]]),
                waits=[td1], sem="cc", n=1)
            td2 = S.emit("g", lambda e, tau=tau: e.dma_start(
                out=xdf[:], in_=xd_out[tau][0]), waits=[tcc] + xdep["xdf"])
            td3 = S.emit("g", lambda e, tau=tau: e.dma_start(
                out=xdf2[:], in_=xd_out[tau][1]), waits=[tcc])
            tsum = S.emit("v", lambda e: e.tensor_add(xdf[:], xdf[:], xdf2[:]),
                          waits=[td2, td3])
            td4 = S.emit("g", lambda e, tau=tau: e.dma_start(out=xdf_d[tau],
                                                             in_=xdf[:]),
                         waits=[tsum])

            w_dl = []
            for m in range(MI):
                pi = 5 + (m % 2)
                tdt = S.emit("t", lambda e, m=m, pi=pi: e.matmul(
                    P[pi][:], b_wdt[:, m * 128:(m + 1) * 128], xdf[0:DTR, :],
                    start=True, stop=True),
                    waits=[tsum] + WPL + pacq(pi))
                ss, sap, sw = rb_sq.acquire()
                te1 = S.emit("s", lambda e, sap=sap, m=m, pi=pi: e.activation(
                    sap, P[pi][:], AF.Exp, bias=b_dtb[:, m:m + 1]),
                    waits=[tdt] + WPL + sw)
                pdeps[pi].append(te1)
                tln = S.emit("s", lambda e, sap=sap, m=m: e.activation(
                    bdl[:, m * TC:(m + 1) * TC], sap, AF.Ln,
                    bias=b_cst[:, 1:2]), waits=WPL + [te1] + xdep["bdl"])
                rb_sq.used(ss, tln)
                tdu = S.emit("v", lambda e, m=m: e.tensor_mul(
                    bdu[:, m * TC:(m + 1) * TC], bdl[:, m * TC:(m + 1) * TC],
                    bu[:, m * TC:(m + 1) * TC]), waits=[tln])
                w_dl.append((tln, tdu))

            for n in range(NST):
                bs_, bB, wB = rb_bcB.acquire()
                tB = S.emit("sy", lambda e, bB=bB, n=n, tau=tau: e.dma_start(
                    out=bB, in_=xdf_d[tau][DTR + n:DTR + n + 1, :]
                    .to_broadcast((128, TC))), waits=wB + [td4])
                cs_, bC, wC = rb_bcC.acquire()
                tC = S.emit("sy", lambda e, bC=bC, n=n, tau=tau: e.dma_start(
                    out=bC, in_=xdf_d[tau][DTR + NST + n:DTR + NST + n + 1, :]
                    .to_broadcast((128, TC))), waits=wC + [td4])
                for m in range(MI):
                    tln, tdu = w_dl[m]
                    col = m * NST + n
                    es, eap, ew = rb_exp.acquire()
                    texp = S.emit("s", lambda e, eap=eap, m=m, col=col:
                                  e.activation(
                        eap, bdl[:, m * TC:(m + 1) * TC], AF.Exp,
                        scale=b_A[:, col:col + 1]),
                        waits=ew + [tln, WPV])
                    ds_, dap, dw = rb_dbu.acquire()
                    tdbu = S.emit("v", lambda e, dap=dap, bB=bB, m=m: e.tensor_mul(
                        dap, bdu[:, m * TC:(m + 1) * TC], bB),
                        waits=dw + [tdu, tB])
                    rb_bcB.used(bs_, tdbu)
                    tscan = S.emit("v", lambda e, eap=eap, dap=dap, col=col:
                                   e.tensor_tensor_scan(
                                       eap, eap, dap, b_state[:, col:col + 1],
                                       ALU.mult, ALU.add),
                                   waits=[texp, tdbu, state_tags.get(col)])
                    rb_dbu.used(ds_, tscan)
                    state_tags[col] = S.emit(
                        "v", lambda e, eap=eap, col=col: e.tensor_copy(
                            b_state[:, col:col + 1], eap[:, TC - 1:TC]),
                        waits=[tscan])
                    if n == 0:
                        ty = S.emit("v", lambda e, eap=eap, bC=bC, m=m:
                                    e.tensor_mul(
                            by[:, m * TC:(m + 1) * TC], eap, bC),
                            waits=[tC, tscan, ty_tags.get(m)])
                        rb_exp.used(es, ty)
                        rb_bcC.used(cs_, ty)
                    else:
                        ds2, dap2, dw2 = rb_dbu.acquire()
                        tsc = S.emit("v", lambda e, eap=eap, dap2=dap2, bC=bC:
                                     e.tensor_mul(dap2, eap, bC),
                                     waits=dw2 + [tC, tscan])
                        rb_exp.used(es, tsc)
                        rb_bcC.used(cs_, tsc)
                        ty = S.emit("v", lambda e, dap2=dap2, m=m: e.tensor_add(
                            by[:, m * TC:(m + 1) * TC],
                            by[:, m * TC:(m + 1) * TC], dap2),
                            waits=[tsc, ty_tags.get(m)])
                        rb_dbu.used(ds2, ty)
                    ty_tags[m] = ty

            w_y = [None] * MI
            for m in range(MI):
                typ = S.emit("v", lambda e, m=m: e.scalar_tensor_tensor(
                    by[:, m * TC:(m + 1) * TC], bu[:, m * TC:(m + 1) * TC],
                    b_dv[:, m:m + 1], by[:, m * TC:(m + 1) * TC],
                    ALU.mult, ALU.add), waits=WPL + [ty_tags.get(m)])
                w_y[m] = S.emit("v", lambda e, m=m: e.tensor_mul(
                    by[:, m * TC:(m + 1) * TC], by[:, m * TC:(m + 1) * TC],
                    bsr[:, m * TC:(m + 1) * TC]), waits=[typ])

            w_pout = []
            t_outproj_last = None
            for mog in range(2):
                chain_w = [pacq(4 + q) for q in range(4)]
                tags4 = [None] * 4
                for m in range(MI):
                    ws, wap, ww = rb_w.acquire()
                    tw = S.emit("sy", lambda e, wap=wap, m=m, mog=mog: e.dma_start(
                        out=wap, in_=w_out[m * 128:(m + 1) * 128,
                                          mog * 512:(mog + 1) * 512]),
                        waits=ww)
                    for q in range(4):
                        tags4[q] = S.emit("t", lambda e, wap=wap, q=q, m=m:
                                          e.matmul(
                            P[4 + q][:], wap[:, q * 128:(q + 1) * 128],
                            by[:, m * TC:(m + 1) * TC],
                            start=(m == 0), stop=(m == MI - 1)),
                            waits=[tw, w_y[m]] + (chain_w[q] if m == 0 else []))
                        rb_w.used(ws, tags4[q])
                t_outproj_last = tags4[3]
                for q in range(4):
                    mo = mog * 4 + q
                    t = S.emit("v", lambda e, mo=mo, q=q: e.scalar_tensor_tensor(
                        bdl[:, mo * TC:(mo + 1) * TC],
                        bx[:, mo * TC:(mo + 1) * TC],
                        0.5, P[4 + q][:], ALU.mult, ALU.add),
                        waits=[tags4[q]])
                    pdeps[4 + q].append(t)
                    w_pout.append(t)
            tpo = S.emit("g", lambda e, tau=tau: e.dma_start(
                out=p_in[tau].rearrange("(dc p) c -> p dc c", p=128),
                in_=bdl[:].rearrange("p (dc c) -> p dc c", c=TC)),
                waits=w_pout)
            S.emit("g", lambda e, tau=tau: e.collective_compute(
                "AllGather", ALU.bypass, replica_groups=RG,
                ins=[p_in[tau]], outs=[p_out[tau]]),
                waits=[tpo], sem="cc", n=1)
            xdep["bx"] = w_pout
            xdep["bh"] = [t_inproj_last]
            xdep["bu"] = [S.now("v")]
            xdep["bsr"] = [w_y[7]]
            xdep["bdl"] = [tpo]
            xdep["xdl"] = [td1]
            xdep["xdf"] = [tdt]
            for m in range(MI):
                ty_tags[m] = t_outproj_last

        _mctx.close()
        bXn = sb("bXn", [128, DC * PT])
        racc = sb("racc", [128, DC * PT])
        att0 = sb("att0", [128, DC * TC])
        att1 = sb("att1", [128, DC * TC])
        bW1 = sb("bW1", [128, 2 * 8 * HF])
        bW2 = sb("bW2", [128, HC * D])
        rb_w1 = Rot([bW1[:, 0:8 * HF], bW1[:, 8 * HF:2 * 8 * HF]])
        bHb = sb("bHb", [128, HC * TC])
        cwbc = sb("cwbc", [128, 2 * TC])
        blg = sb("blg", [NE, TC])
        btl = sb("btl", [128, 2 * NE])
        bts = sb("bts", [128, 8 * NE])
        bcw = sb("bcw", [NE, PT])
        rb_cwb = Rot([cwbc[:, 0:TC], cwbc[:, TC:2 * TC]])

        S.barrier()

        ffn_dep = {"att0": [], "att1": [], "bXn": [], "racc": [], "bcw": []}
        for pp in range(NPASS):
            w_x1 = None
            for hh in range(2):
                cc_i = 2 * pp + hh
                wait_cc = ("cc", 2 * (cc_i + 1))
                ta0 = S.emit("sy", lambda e, cc_i=cc_i: e.dma_start(
                    out=att0[:].rearrange("p (dc c) -> p dc c", c=TC),
                    in_=p_out[cc_i][0].rearrange("(dc p) c -> p dc c", p=128)),
                    waits=[wait_cc, S.now("t"), S.now("v"), S.now("s")]
                    + ffn_dep["att0"])
                ta1 = S.emit("sy", lambda e, cc_i=cc_i: e.dma_start(
                    out=att1[:].rearrange("p (dc c) -> p dc c", c=TC),
                    in_=p_out[cc_i][1].rearrange("(dc p) c -> p dc c", p=128)),
                    waits=[wait_cc, S.now("t"), S.now("v")] + ffn_dep["att1"])
                tadd = None
                for k in range(DC):
                    tadd = S.emit("v", lambda e, k=k: e.tensor_add(
                        att0[:, k * TC:(k + 1) * TC],
                        att0[:, k * TC:(k + 1) * TC],
                        att1[:, k * TC:(k + 1) * TC]), waits=[ta0, ta1])
                tmb, trb = emit_ln_stats(att0, TC, 0, 1, 2, 3, [tadd])
                wxf = []
                for k in range(DC):
                    wxf.append(S.emit("sy", lambda e, k=k, cc_i=cc_i: e.dma_start(
                        out=att1[:, k * TC:(k + 1) * TC],
                        in_=xT[k * 128:(k + 1) * 128,
                               cc_i * TC:(cc_i + 1) * TC]), waits=[tadd]))
                w_x1c = []
                for k in range(DC):
                    ts_ = S.emit("v", lambda e, k=k: e.tensor_tensor(
                        att0[:, k * TC:(k + 1) * TC],
                        att0[:, k * TC:(k + 1) * TC],
                        P[2][:], ALU.subtract), waits=[tmb])
                    tm_ = S.emit("v", lambda e, k=k: e.tensor_mul(
                        att0[:, k * TC:(k + 1) * TC],
                        att0[:, k * TC:(k + 1) * TC], P[3][:]),
                        waits=[trb, ts_])
                    t_ = S.emit("v", lambda e, k=k: e.tensor_add(
                        att0[:, k * TC:(k + 1) * TC],
                        att0[:, k * TC:(k + 1) * TC],
                        att1[:, k * TC:(k + 1) * TC]), waits=wxf + [tm_])
                    w_x1c.append(t_)
                pdeps[2].append(w_x1c[-1])
                pdeps[3].append(w_x1c[-1])
                tx1d = S.emit("g", lambda e, pp=pp, hh=hh: e.dma_start(
                    out=x1_d[pp].rearrange("(dc p) c -> p dc c", p=128)
                    [:, :, hh * TC:(hh + 1) * TC],
                    in_=att0[:].rearrange("p (dc c) -> p dc c", c=TC)),
                    waits=w_x1c)
                tmb, trb = emit_ln_stats(att0, TC, 0, 1, 2, 3, w_x1c)
                for k in range(DC):
                    ts_ = S.emit("v", lambda e, k=k: e.tensor_tensor(
                        att0[:, k * TC:(k + 1) * TC],
                        att0[:, k * TC:(k + 1) * TC],
                        P[2][:], ALU.subtract), waits=[tmb, tx1d])
                    w_x1 = S.emit("v", lambda e, k=k, hh=hh: e.tensor_mul(
                        bXn[:, k * PT + hh * TC:k * PT + (hh + 1) * TC],
                        att0[:, k * TC:(k + 1) * TC], P[3][:]),
                        waits=[trb, ts_] + ffn_dep["bXn"])
                pdeps[2].append(w_x1)
                pdeps[3].append(w_x1)

            for tt in range(2):
                tlg = None
                wlg = pacq(4)
                for k in range(DC):
                    tlg = S.emit("t", lambda e, k=k, tt=tt: e.matmul(
                        P[4][0:NE, :], b_rw[:, k * NE:(k + 1) * NE],
                        bXn[:, k * PT + tt * TC:k * PT + (tt + 1) * TC],
                        start=(k == 0), stop=(k == DC - 1)),
                        waits=[w_x1] + WPL + (wlg if k == 0 else []))
                tb = S.emit("v", lambda e: e.tensor_scalar(
                    blg[:], P[4][0:NE, :], b_rb[:, 0:1], None, ALU.add),
                    waits=[tlg] + WPL)
                pdeps[4].append(tb)
                for g in range(NG):
                    pt_i = 5 if g % 2 == 0 else 6
                    ttr = S.emit("t", lambda e, g=g, pt_i=pt_i: e.transpose(
                        P[pt_i][:, 0:NE], blg[:, g * 128:(g + 1) * 128],
                        b_id15[:]), waits=[tb] + WPL + pacq(pt_i))
                    lt = btl[:, (g % 2) * NE:(g % 2 + 1) * NE]
                    tcp = S.emit("v", lambda e, lt=lt, pt_i=pt_i: e.tensor_copy(
                        lt, P[pt_i][:, 0:NE]), waits=[ttr])
                    pdeps[pt_i].append(tcp)
                    mx, sm_, m1, m2, m3 = (bts[:, i:i + 1] for i in range(5))
                    prb = bts[:, NE:2 * NE]
                    t1b = bts[:, 2 * NE:3 * NE]
                    t2b = bts[:, 3 * NE:4 * NE]
                    mskb = bts[:, 4 * NE:5 * NE]
                    S.emit("v", lambda e, lt=lt, mx=mx: e.reduce_max(
                        mx, lt, axis=AX.X), waits=[tcp])
                    S.emit("v", lambda e, lt=lt, mx=mx: e.tensor_scalar(
                        lt, lt, mx, None, ALU.subtract), waits=[S.now("v")])
                    te = S.emit("s", lambda e, lt=lt: e.activation(
                        lt, lt, AF.Exp), waits=[S.now("v")])
                    S.emit("v", lambda e, lt=lt, sm_=sm_: e.reduce_sum(
                        sm_, lt, axis=AX.X), waits=[te])
                    S.emit("v", lambda e, sm_=sm_: e.reciprocal(sm_, sm_),
                           waits=[S.now("v")])
                    S.emit("v", lambda e, lt=lt, sm_=sm_, prb=prb: e.tensor_scalar(
                        prb, lt, sm_, None, ALU.mult), waits=[S.now("v")])
                    S.emit("v", lambda e, prb=prb, m1=m1: e.reduce_max(
                        m1, prb, axis=AX.X), waits=[S.now("v")])
                    S.emit("v", lambda e, prb=prb, m1=m1, mskb=mskb:
                           e.tensor_scalar(mskb, prb, m1, None, ALU.is_lt),
                           waits=[S.now("v")])
                    S.emit("v", lambda e, prb=prb, mskb=mskb, t1b=t1b:
                           e.tensor_mul(t1b, prb, mskb), waits=[S.now("v")])
                    S.emit("v", lambda e, t1b=t1b, m2=m2: e.reduce_max(
                        m2, t1b, axis=AX.X), waits=[S.now("v")])
                    S.emit("v", lambda e, t1b=t1b, m2=m2, mskb=mskb:
                           e.tensor_scalar(mskb, t1b, m2, None, ALU.is_lt),
                           waits=[S.now("v")])
                    S.emit("v", lambda e, t1b=t1b, mskb=mskb, t2b=t2b:
                           e.tensor_mul(t2b, t1b, mskb), waits=[S.now("v")])
                    S.emit("v", lambda e, t2b=t2b, m3=m3: e.reduce_max(
                        m3, t2b, axis=AX.X), waits=[S.now("v")])
                    S.emit("v", lambda e, prb=prb, m3=m3, mskb=mskb:
                           e.tensor_scalar(mskb, prb, m3, None, ALU.is_ge),
                           waits=[S.now("v")])
                    tcw = S.emit("v", lambda e, lt=lt, prb=prb, mskb=mskb:
                                 e.tensor_mul(lt, prb, mskb),
                                 waits=[S.now("v")])
                    ttc = S.emit("t", lambda e, lt=lt: e.transpose(
                        P[7][0:NE, 0:128], lt, b_id128[:]),
                        waits=[tcw] + WPL + pacq(7))
                    tcc2 = S.emit("v", lambda e, g=g, tt=tt: e.tensor_copy(
                        bcw[:, tt * TC + g * 128:tt * TC + (g + 1) * 128],
                        P[7][0:NE, 0:128]), waits=[ttc] + ffn_dep["bcw"])
                    pdeps[7].append(tcc2)
            w_cw = S.now("v")
            tcwd = S.emit("g", lambda e, pp=pp: e.dma_start(out=cw_d[pp],
                                                            in_=bcw[:]),
                          waits=[w_cw])

            racc_tags = {}
            for tt in range(2):
                for dc in range(DC):
                    pi = 2 + (dc % 2)
                    ti = S.emit("t", lambda e, dc=dc, tt=tt, pi=pi: e.matmul(
                        P[pi][:], b_eb2h[:, dc * 128:(dc + 1) * 128],
                        bcw[:, tt * TC:(tt + 1) * TC],
                        start=True, stop=True),
                        waits=[w_cw, WPV] + pacq(pi))
                    t = S.emit("v", lambda e, dc=dc, tt=tt, pi=pi: e.tensor_scalar(
                        racc[:, dc * PT + tt * TC:dc * PT + (tt + 1) * TC],
                        P[pi][:], b_sb2h[:, dc:dc + 1], None, ALU.add),
                        waits=[ti, WPV] + ffn_dep["racc"])
                    pdeps[pi].append(t)
                    racc_tags[(tt, dc)] = t

            for e_i in range(NE + 1):
                w1s, w1ap, w1w = rb_w1.acquire()
                src1 = (e_w1[e_i] if e_i < NE else sw1)
                src2 = (e_w2[e_i] if e_i < NE else sw2)
                tw1 = S.emit("sy", lambda e, w1ap=w1ap, src1=src1: e.dma_start(
                    out=w1ap.rearrange("p (k h) -> p k h", h=HF),
                    in_=src1.rearrange("(k p) h -> p k h", p=128)),
                    waits=w1w)
                tw2 = S.emit("sy", lambda e, src2=src2: e.dma_start(
                    out=bW2[:].rearrange("p (k d) -> p k d", d=D),
                    in_=src2.rearrange("(k p) d -> p k d", p=128)),
                    waits=[S.now("t")])
                for tt in range(2):
                    w_bh = []
                    for hc in range(HC):
                        pi = hc % 2
                        tH = None
                        wH = pacq(pi)
                        for k in range(DC):
                            tH = S.emit("t", lambda e, w1ap=w1ap, hc=hc, k=k,
                                        tt=tt, pi=pi: e.matmul(
                                P[pi][:],
                                w1ap[:, k * HF + hc * 128:k * HF + (hc + 1) * 128],
                                bXn[:, k * PT + tt * TC:k * PT + (tt + 1) * TC],
                                start=(k == 0), stop=(k == DC - 1)),
                                waits=[tw1, w_x1] + (wH if k == 0 else []))
                            rb_w1.used(w1s, tH)
                        if e_i < NE:
                            bias = b_eb1[:, e_i * HC + hc:e_i * HC + hc + 1]
                            scr = att0[:, hc * TC:(hc + 1) * TC]
                            tr = S.emit("s", lambda e, scr=scr, pi=pi, bias=bias:
                                        e.activation(scr, P[pi][:], AF.Relu,
                                                     bias=bias),
                                        waits=[tH] + WPL)
                            pdeps[pi].append(tr)
                            cbs, cbap, cbw = rb_cwb.acquire()
                            tbc = S.emit("sy", lambda e, cbap=cbap, e_i=e_i,
                                         tt=tt, pp=pp: e.dma_start(
                                out=cbap,
                                in_=cw_d[pp][e_i:e_i + 1, tt * TC:(tt + 1) * TC]
                                .to_broadcast((128, TC))),
                                waits=cbw + [tcwd])
                            tm = S.emit("v", lambda e, scr=scr, cbap=cbap, hc=hc:
                                        e.tensor_mul(
                                bHb[:, hc * TC:(hc + 1) * TC], scr, cbap),
                                waits=[tr, tbc])
                            rb_cwb.used(cbs, tm)
                            w_bh.append(tm)
                        else:
                            bias = b_sb1[:, hc:hc + 1]
                            tr = S.emit("s", lambda e, hc=hc, pi=pi, bias=bias:
                                        e.activation(
                                bHb[:, hc * TC:(hc + 1) * TC], P[pi][:],
                                AF.Relu, bias=bias), waits=[tH] + WPL)
                            pdeps[pi].append(tr)
                            w_bh.append(tr)
                    for dc in range(DC):
                        pi = 2 + (dc % 2)
                        tR = None
                        wR = pacq(pi)
                        for hk in range(HC):
                            tR = S.emit("t", lambda e, dc=dc, hk=hk, pi=pi:
                                        e.matmul(
                                P[pi][:],
                                bW2[:, hk * D + dc * 128:hk * D + (dc + 1) * 128],
                                bHb[:, hk * TC:(hk + 1) * TC],
                                start=(hk == 0), stop=(hk == HC - 1)),
                                waits=[tw2] + w_bh + (wR if hk == 0 else []))
                        t = S.emit("v", lambda e, dc=dc, tt=tt, pi=pi:
                                   e.scalar_tensor_tensor(
                            racc[:, dc * PT + tt * TC:dc * PT + (tt + 1) * TC],
                            P[pi][:], 1.0,
                            racc[:, dc * PT + tt * TC:dc * PT + (tt + 1) * TC],
                            ALU.mult, ALU.add),
                            waits=[tR, racc_tags.get((tt, dc))])
                        pdeps[pi].append(t)
                        racc_tags[(tt, dc)] = t
            w_racc = S.now("v")

            trt = S.emit("g", lambda e, pp=pp: e.dma_start(
                out=rt_in[pp].rearrange("(dc p) c -> p dc c", p=128),
                in_=racc[:].rearrange("p (dc c) -> p dc c", c=PT)),
                waits=[w_racc])
            trc = S.emit("g", lambda e, pp=pp: e.collective_compute(
                "AllGather", ALU.bypass, replica_groups=RG,
                ins=[rt_in[pp]], outs=[rt_out[pp]]),
                waits=[trt], sem="cc", n=1)

            tout = None
            for hh in range(2):
                tf0 = S.emit("sy", lambda e, pp=pp, hh=hh: e.dma_start(
                    out=att0[:].rearrange("p (dc c) -> p dc c", c=TC),
                    in_=rt_out[pp][0].rearrange("(dc p) c -> p dc c", p=128)
                    [:, :, hh * TC:(hh + 1) * TC]),
                    waits=[trc, S.now("s"), S.now("t"), S.now("v"), tout])
                tf1 = S.emit("sy", lambda e, pp=pp, hh=hh: e.dma_start(
                    out=att1[:].rearrange("p (dc c) -> p dc c", c=TC),
                    in_=rt_out[pp][1].rearrange("(dc p) c -> p dc c", p=128)
                    [:, :, hh * TC:(hh + 1) * TC]),
                    waits=[trc, S.now("t"), S.now("v"), tout])
                tadd = None
                for k in range(DC):
                    tadd = S.emit("v", lambda e, k=k: e.tensor_add(
                        att0[:, k * TC:(k + 1) * TC],
                        att0[:, k * TC:(k + 1) * TC],
                        att1[:, k * TC:(k + 1) * TC]), waits=[tf0, tf1])
                tmb, trb = emit_ln_stats(att0, TC, 0, 1, 2, 3, [tadd])
                tx1l = S.emit("sy", lambda e, pp=pp, hh=hh: e.dma_start(
                    out=att1[:].rearrange("p (dc c) -> p dc c", c=TC),
                    in_=x1_d[pp].rearrange("(dc p) c -> p dc c", p=128)
                    [:, :, hh * TC:(hh + 1) * TC]), waits=[tadd])
                w_fin = []
                for k in range(DC):
                    ts_ = S.emit("v", lambda e, k=k: e.tensor_tensor(
                        att0[:, k * TC:(k + 1) * TC],
                        att0[:, k * TC:(k + 1) * TC],
                        P[2][:], ALU.subtract), waits=[tmb])
                    tm_ = S.emit("v", lambda e, k=k: e.tensor_mul(
                        att0[:, k * TC:(k + 1) * TC],
                        att0[:, k * TC:(k + 1) * TC], P[3][:]),
                        waits=[trb, ts_])
                    t = S.emit("v", lambda e, k=k: e.tensor_add(
                        att1[:, k * TC:(k + 1) * TC],
                        att0[:, k * TC:(k + 1) * TC],
                        att1[:, k * TC:(k + 1) * TC]),
                        waits=[tm_, tx1l])
                    w_fin.append(t)
                pdeps[2].append(w_fin[-1])
                pdeps[3].append(w_fin[-1])
                tout = S.emit("g", lambda e, pp=pp, hh=hh: e.dma_start(
                    out=out.rearrange("(dc p) c -> p dc c", p=128)
                    [:, :, (2 * pp + hh) * TC:(2 * pp + hh + 1) * TC],
                    in_=att1[:].rearrange("p (dc c) -> p dc c", c=TC)),
                    waits=w_fin)
            ffn_dep["att0"] = [S.now("v")]
            ffn_dep["att1"] = [tout]
            ffn_dep["bXn"] = [S.now("t")]
            ffn_dep["racc"] = [trt]
            ffn_dep["bcw"] = [tcwd, S.now("t")]

        S.emit("g", lambda e: e.nop(),
               waits=[S.now("dg0"), S.now("dg1")], sem=None)

        def replay(h, name):
            for rw, fn, sem, n in S.ops[name]:
                for sname, v in rw:
                    h.wait_ge(SEM[sname], v)
                r = fn(h)
                if sem is not None:
                    r.then_inc(SEM[sem], n)

        @block.sync
        def _(sync):
            replay(sync, "sy")

        @block.tensor
        def _(tensor):
            replay(tensor, "t")

        @block.vector
        def _(vector):
            replay(vector, "v")

        @block.scalar
        def _(scalar):
            replay(scalar, "s")

        @block.gpsimd
        def _(gpsimd):
            replay(gpsimd, "g")

    return nc


def _prep_core_inputs(inp, b, j):
    f32 = np.float32
    x = np.asarray(inp["L0_attn_x"], dtype=f32)
    xT = np.ascontiguousarray(x[b].T)

    w_inT = np.asarray(inp["L0_attn_in_proj_w"], dtype=f32).T
    w_in = np.ascontiguousarray(np.concatenate(
        [w_inT[:, j * DIH:(j + 1) * DIH],
         w_inT[:, DI + j * DIH:DI + (j + 1) * DIH]], axis=1))

    w_xp = np.ascontiguousarray(
        np.asarray(inp["L0_attn_x_proj_w"], dtype=f32).T[j * DIH:(j + 1) * DIH])
    w_dt = np.ascontiguousarray(
        np.asarray(inp["L0_attn_dt_proj_w"], dtype=f32).T[:, j * DIH:(j + 1) * DIH])

    def pack_pp(vec, nt):
        return np.ascontiguousarray(vec.reshape(nt, 128).T)

    dt_b = pack_pp(np.asarray(inp["L0_attn_dt_proj_b"], dtype=f32)
                   [j * DIH:(j + 1) * DIH], MI)
    cvw = np.asarray(inp["L0_attn_conv_w"], dtype=f32)[:, 0, :][j * DIH:(j + 1) * DIH]
    cv_w = np.ascontiguousarray(
        cvw.reshape(MI, 128, 4).transpose(1, 0, 2).reshape(128, MI * 4))
    cv_b = pack_pp(np.asarray(inp["L0_attn_conv_b"], dtype=f32)
                   [j * DIH:(j + 1) * DIH], MI)
    al = np.asarray(inp["L0_attn_A_log"], dtype=f32)[j * DIH:(j + 1) * DIH]
    a_log = np.ascontiguousarray(
        al.reshape(MI, 128, NST).transpose(1, 0, 2).reshape(128, MI * NST))
    d_vec = pack_pp(np.asarray(inp["L0_attn_D"], dtype=f32)[j * DIH:(j + 1) * DIH], MI)
    nw = pack_pp(np.asarray(inp["L0_attn_norm_w"], dtype=f32), DC)
    w_out = np.ascontiguousarray(
        np.asarray(inp["L0_attn_out_proj_w"], dtype=f32).T[j * DIH:(j + 1) * DIH])

    r_w = np.ascontiguousarray(np.asarray(inp["L0_ffn_router_W"], dtype=f32))
    r_b = np.ascontiguousarray(
        np.asarray(inp["L0_ffn_router_b"], dtype=f32).reshape(NE, 1))
    sw1 = np.ascontiguousarray(
        np.asarray(inp["L0_ffn_sW1"], dtype=f32)[:, j * HF:(j + 1) * HF])
    sb1 = pack_pp(np.asarray(inp["L0_ffn_sb1"], dtype=f32)[j * HF:(j + 1) * HF], HC)
    sw2 = np.ascontiguousarray(
        np.asarray(inp["L0_ffn_sW2"], dtype=f32)[j * HF:(j + 1) * HF])
    sb2 = pack_pp(np.asarray(inp["L0_ffn_sb2"], dtype=f32), DC)
    e_w1 = np.ascontiguousarray(
        np.asarray(inp["L0_ffn_eW1"], dtype=f32)[:, :, j * HF:(j + 1) * HF])
    eb1 = np.asarray(inp["L0_ffn_eb1"], dtype=f32)[:, j * HF:(j + 1) * HF]
    e_b1 = np.ascontiguousarray(
        eb1.reshape(NE, HC, 128).transpose(2, 0, 1).reshape(128, NE * HC))
    e_w2 = np.ascontiguousarray(
        np.asarray(inp["L0_ffn_eW2"], dtype=f32)[:, j * HF:(j + 1) * HF])
    e_b2 = np.ascontiguousarray(np.asarray(inp["L0_ffn_eb2"], dtype=f32))

    return dict(
        xT=xT, w_in=w_in, w_xp=w_xp, w_dt=w_dt, dt_b=dt_b, cv_w=cv_w, cv_b=cv_b,
        a_log=a_log, d_vec=d_vec, nw=nw, w_out=w_out, r_w=r_w, r_b=r_b,
        sw1=sw1, sb1=sb1, sw2=sw2, sb2=sb2, e_w1=e_w1, e_b1=e_b1, e_w2=e_w2,
        e_b2=e_b2,
        ones1=np.ones((1, 128), dtype=f32),
        onesd=np.full((128, 1), 1.0 / D, dtype=f32),
        id15=np.eye(NE, dtype=f32),
        id128=np.eye(128, dtype=f32),
        cst=np.stack([np.full(128, EPS, dtype=f32),
                      np.ones(128, dtype=f32)], axis=1),
    )


def kernel(**inp):
    TOK = np.asarray(inp["L0_attn_x"]).shape[1]
    nc = build_nc(TOK=TOK, TC=512)
    in_maps = [_prep_core_inputs(inp, c // 2, c % 2) for c in range(8)]
    res = run_bass_kernel_spmd(nc, in_maps, core_ids=list(range(8)))
    outs = [np.asarray(res.results[2 * b]["out"]).T for b in range(4)]
    return np.stack(outs).astype(np.float32)


# version 04 (final)
# speedup vs baseline: 1.0845x; 1.0390x over previous
import numpy as np
import concourse.bass as bass
import concourse.mybir as mybir
from concourse.bass_utils import run_bass_kernel_spmd

F32 = mybir.dt.float32
AF = mybir.ActivationFunctionType
ALU = mybir.AluOpType
AX = mybir.AxisListType

D = 1024
DI = 2048
DIH = 1024
NST = 16
DTR = 64
NE = 15
HF = 512
DC = 8
MI = 8
HC = HF // 128
XD = DTR + 2 * NST
EPS = 1e-5

ENGS = ("sy", "t", "v", "s", "g")
DMA_LANES = {"sy": ("ds0", "ds1", "ds2", "ds3"), "g": ("dg0", "dg1")}
ALL_SEMS = ("ds0", "ds1", "ds2", "ds3", "dg0", "dg1", "t", "v", "s", "cc")


class Sched:

    def __init__(self):
        self.ops = {e: [] for e in ENGS}
        self.cnt = {k: 0 for k in ALL_SEMS}
        self.lastw = {}
        self.rr = {"sy": 0, "g": 0}

    def emit(self, eng, fn, waits=(), sem="default", n=None):
        waits = [w for w in waits if w is not None]
        if sem == "default":
            if eng in DMA_LANES:
                lanes = DMA_LANES[eng]
                sem = lanes[self.rr[eng] % len(lanes)]
                self.rr[eng] += 1
                n = 16
                waits = waits + [(sem, self.cnt[sem])]
            else:
                sem, n = {"t": ("t", 1), "v": ("v", 1), "s": ("s", 1)}[eng]
        elif sem == "cc":
            waits = waits + [("cc", self.cnt["cc"])]
        best = {}
        for w in waits:
            if w is None:
                continue
            s, v = w
            assert v <= self.cnt[s], f"wait on future {s}:{v} > {self.cnt[s]}"
            if best.get(s, 0) < v:
                best[s] = v
        rw = []
        for s, v in best.items():
            key = (eng, s)
            if self.lastw.get(key, 0) < v:
                self.lastw[key] = v
                rw.append((s, v))
        tag = None
        if sem is not None:
            self.cnt[sem] += n
            tag = (sem, self.cnt[sem])
        self.ops[eng].append((tuple(rw), fn, sem, n))
        return tag

    def now(self, sem):
        return (sem, self.cnt[sem])

    def barrier(self):
        snap = [(s, self.cnt[s]) for s in self.cnt if self.cnt[s] > 0]
        for e in ENGS:
            self.emit(e, lambda h: h.nop(), waits=snap, sem=None)


class Rot:

    def __init__(self, aps):
        self.aps = list(aps)
        self.deps = [[] for _ in self.aps]
        self.i = -1

    def acquire(self):
        self.i += 1
        s = self.i % len(self.aps)
        w = self.deps[s]
        self.deps[s] = []
        return s, self.aps[s], w

    def used(self, s, tag):
        if tag is not None:
            self.deps[s].append(tag)


def build_nc(TOK=4096, TC=512):
    NCH = TOK // TC
    NPASS = NCH // 2
    PT = 2 * TC
    NG = TC // 128

    nc = bass.Bass("TRN2", target_bir_lowering=False)

    def di(name, shape):
        return nc.dram_tensor(name, shape, F32, kind="ExternalInput").ap()

    xT = di("xT", [D, TOK])
    w_in = di("w_in", [D, DI])
    w_xp = di("w_xp", [DIH, XD])
    w_dt = di("w_dt", [DTR, DIH])
    dt_b = di("dt_b", [128, MI])
    cv_w = di("cv_w", [128, MI * 4])
    cv_b = di("cv_b", [128, MI])
    a_log = di("a_log", [128, MI * NST])
    d_vec = di("d_vec", [128, MI])
    nw = di("nw", [128, DC])
    w_out = di("w_out", [DIH, D])
    r_w = di("r_w", [D, NE])
    r_b = di("r_b", [NE, 1])
    sw1 = di("sw1", [D, HF])
    sb1 = di("sb1", [128, HC])
    sw2 = di("sw2", [HF, D])
    sb2 = di("sb2", [128, DC])
    e_w1 = di("e_w1", [NE, D, HF])
    e_b1 = di("e_b1", [128, NE * HC])
    e_w2 = di("e_w2", [NE, HF, D])
    e_b2 = di("e_b2", [NE, D])
    ones1 = di("ones1", [1, 128])
    onesd = di("onesd", [128, 1])
    id15 = di("id15", [NE, NE])
    id128 = di("id128", [128, 128])
    cst = di("cst", [128, 2])

    out = nc.dram_tensor("out", [D, TOK], F32, kind="ExternalOutput").ap()

    xd_in = nc.dram_tensor("xd_in", [NCH, XD, TC], F32).ap()
    xd_out = nc.dram_tensor("xd_out", [NCH, 2, XD, TC], F32).ap()
    xdf_d = nc.dram_tensor("xdf_d", [NCH, XD, TC], F32).ap()
    p_in = nc.dram_tensor("p_in", [NCH, D, TC], F32).ap()
    p_out = nc.dram_tensor("p_out", [NCH, 2, D, TC], F32).ap()
    cw_d = nc.dram_tensor("cw_d", [NPASS, NE, PT], F32).ap()
    rt_in = nc.dram_tensor("rt_in", [NPASS, D, PT], F32).ap()
    rt_out = nc.dram_tensor("rt_out", [NPASS, 2, D, PT], F32).ap()
    x1_d = nc.dram_tensor("x1_d", [NPASS, D, PT], F32).ap()

    S = Sched()
    RG = [[0, 1], [2, 3], [4, 5], [6, 7]]

    from contextlib import ExitStack
    with ExitStack() as _ctx:
        def sb(name, shape):
            return _ctx.enter_context(nc.sbuf_tensor(name, shape, F32))

        b_dtb = sb("b_dtb", [128, MI])
        b_cvw = sb("b_cvw", [128, MI * 4])
        b_cvb = sb("b_cvb", [128, MI])
        b_A = sb("b_A", [128, MI * NST])
        b_dv = sb("b_dv", [128, MI])
        b_nw = sb("b_nw", [128, DC])
        b_ones1 = sb("b_ones1", [1, 128])
        b_onesd = sb("b_onesd", [128, 1])
        b_id15 = sb("b_id15", [NE, NE])
        b_id128 = sb("b_id128", [128, 128])
        b_rb = sb("b_rb", [NE, 1])
        b_sb1 = sb("b_sb1", [128, HC])
        b_sb2h = sb("b_sb2h", [128, DC])
        b_eb1 = sb("b_eb1", [128, NE * HC])
        b_eb2h = sb("b_eb2h", [NE, D])
        b_wxp = sb("b_wxp", [128, MI * XD])
        b_wdt = sb("b_wdt", [DTR, DIH])
        b_rw = sb("b_rw", [128, DC * NE])
        b_state = sb("b_state", [128, MI * NST])
        b_halo = sb("b_halo", [128, MI * 3])
        b_rows = sb("b_rows", [1, 4 * TC])
        b_cst = sb("b_cst", [128, 2])
        bsq = sb("bsq", [128, 2 * TC])
        _mctx = ExitStack()
        def sbm(name, shape):
            return _mctx.enter_context(nc.sbuf_tensor(name, shape, F32))
        bxA = sbm("bxA", [128, DC * TC])
        bxB = sbm("bxB", [128, DC * TC])
        bh = sbm("bh", [128, DC * TC])
        bu = sbm("bu", [128, MI * TC])
        bsr = sbm("bsr", [128, MI * TC])
        bdl = sbm("bdl", [128, MI * TC])
        bdu = sbm("bdu", [128, MI * TC])
        by = sbm("by", [128, MI * TC])
        bexp = sbm("bexp", [128, 3 * TC])
        bdbu = sbm("bdbu", [128, 4 * TC])
        bbcB = sbm("bbcB", [128, 2 * TC])
        bbcC = sbm("bbcC", [128, 2 * TC])
        cbuf = sbm("cbuf", [128, 2 * (TC + 6)])
        wbufA = sbm("wbufA", [128, 4 * 512])
        xdl = sbm("xdl", [XD, TC])
        xdf = sbm("xdf", [XD, TC])
        xdf2 = sbm("xdf2", [XD, TC])
        p0, p1, p2, p3, p4, p5, p6, p7 = (
            _ctx.enter_context(nc.psum_tensor(f"p{i}", [128, TC], F32))
            for i in range(8))
        SEMS = {name: _ctx.enter_context(nc.semaphore("sm_" + name))
                for name in ALL_SEMS}
        block = _ctx.enter_context(nc.Block())
        P = [p0, p1, p2, p3, p4, p5, p6, p7]
        SEM = SEMS
        pdeps = [[] for _ in range(8)]

        def pacq(i):
            w = pdeps[i]
            pdeps[i] = []
            return w

        rb_sq = Rot([bsq[:, 0:TC], bsq[:, TC:2 * TC]])
        rb_w = Rot([wbufA[:, i * 512:(i + 1) * 512] for i in range(4)])
        rb_cb = Rot([cbuf[:, 0:TC + 6], cbuf[:, TC + 6:2 * (TC + 6)]])
        rb_exp = Rot([bexp[:, i * TC:(i + 1) * TC] for i in range(3)])
        rb_dbu = Rot([bdbu[:, i * TC:(i + 1) * TC] for i in range(4)])
        rb_bcB = Rot([bbcB[:, 0:TC], bbcB[:, TC:2 * TC]])
        rb_bcC = Rot([bbcC[:, 0:TC], bbcC[:, TC:2 * TC]])

        def ld(dst, src):
            return S.emit("sy", lambda e, d=dst, s=src: e.dma_start(out=d, in_=s))

        for dst, src in (
            (b_dtb[:], dt_b), (b_cvw[:], cv_w), (b_cvb[:], cv_b),
            (b_A[:], a_log), (b_dv[:], d_vec), (b_nw[:], nw),
            (b_ones1[:], ones1), (b_onesd[:], onesd), (b_id15[:], id15),
            (b_id128[:], id128), (b_rb[:], r_b), (b_sb1[:], sb1),
            (b_sb2h[:], sb2), (b_eb1[:], e_b1), (b_eb2h[:], e_b2),
            (b_wdt[:], w_dt), (b_cst[:], cst),
        ):
            ld(dst, src)
        for m in range(MI):
            ld(b_wxp[:, m * XD:(m + 1) * XD], w_xp[m * 128:(m + 1) * 128, :])
        for k in range(DC):
            ld(b_rw[:, k * NE:(k + 1) * NE], r_w[k * 128:(k + 1) * 128, :])
        WPL = [S.now(l) for l in DMA_LANES["sy"]]
        ta = S.emit("s", lambda e: e.activation(b_A[:], b_A[:], AF.Exp), waits=WPL)
        S.emit("v", lambda e: e.tensor_scalar_mul(b_A[:], b_A[:], -1.0), waits=[ta])
        S.emit("v", lambda e: e.tensor_scalar_mul(b_sb2h[:], b_sb2h[:], 0.5),
               waits=WPL)
        S.emit("v", lambda e: e.tensor_scalar_mul(b_eb2h[:], b_eb2h[:], 0.5),
               waits=WPL)
        S.emit("v", lambda e: e.memset(b_state[:], 0.0))
        S.emit("v", lambda e: e.memset(b_halo[:], 0.0))
        WPV = S.now("v")

        rows_dep = []

        def emit_ln_stats(src, n_tc, pa, pb, pm, pr, src_ready):
            wA = pacq(pa) + src_ready + WPL
            tagm = None
            for k in range(DC):
                tagm = S.emit("t", lambda e, k=k: e.matmul(
                    P[pa][0:1, :], b_onesd[:], src[:, k * n_tc:(k + 1) * n_tc],
                    start=(k == 0), stop=(k == DC - 1)),
                    waits=(wA if k == 0 else ()))
            wB = pacq(pb)
            tagq = None
            for k in range(DC):
                ss, sap, sw = rb_sq.acquire()
                tsq = S.emit("s", lambda e, sap=sap, k=k: e.activation(
                    sap, src[:, k * n_tc:(k + 1) * n_tc], AF.Square),
                    waits=sw + src_ready)
                tagq = S.emit("t", lambda e, sap=sap, k=k: e.matmul(
                    P[pb][0:1, :], b_onesd[:], sap,
                    start=(k == 0), stop=(k == DC - 1)),
                    waits=[tsq] + (wB if k == 0 else []))
                rb_sq.used(ss, tagq)
            mrow = b_rows[0:1, 0:n_tc]
            vrow = b_rows[0:1, TC:TC + n_tc]
            t0 = S.emit("v", lambda e: e.tensor_copy(mrow, P[pa][0:1, :]),
                        waits=[tagm, tagq] + rows_dep)
            t1 = S.emit("v", lambda e: e.tensor_mul(vrow, mrow, mrow),
                        waits=[t0])
            pdeps[pa] += [t0, t1]
            t2 = S.emit("v", lambda e: e.tensor_tensor(vrow, P[pb][0:1, :], vrow,
                                                       ALU.subtract),
                        waits=[t1])
            pdeps[pb].append(t2)
            t3 = S.emit("s", lambda e: e.activation(vrow, vrow, AF.Sqrt, bias=b_cst[0:1, 0:1]),
                        waits=[t2] + WPL)
            t4 = S.emit("v", lambda e: e.reciprocal(vrow, vrow), waits=[t3])
            tmb = S.emit("t", lambda e: e.matmul(P[pm][:], b_ones1[:], mrow,
                                                 start=True, stop=True),
                         waits=pacq(pm) + [t0] + WPL)
            trb = S.emit("t", lambda e: e.matmul(P[pr][:], b_ones1[:], vrow,
                                                 start=True, stop=True),
                         waits=pacq(pr) + [t4])
            rows_dep[:] = [tmb, trb]
            return tmb, trb

        state_tags = {}
        ty_tags = {}
        xdep = {"bxA": [], "bxB": [], "bh": [], "bu": [], "bsr": [], "bdl": [],
                "by": [], "xdl": [], "xdf": []}

        for tau in range(NCH):
            c0 = tau * TC
            bx = bxA if tau % 2 == 0 else bxB
            bxkey = "bxA" if tau % 2 == 0 else "bxB"
            wx = []
            for k in range(DC):
                wx.append(S.emit("sy", lambda e, k=k, c0=c0, bx=bx: e.dma_start(
                    out=bx[:, k * TC:(k + 1) * TC],
                    in_=xT[k * 128:(k + 1) * 128, c0:c0 + TC]),
                    waits=xdep[bxkey]))
            tmb, trb = emit_ln_stats(bx, TC, 0, 1, 2, 3, wx)
            w_xn = None
            for k in range(DC):
                ts_ = S.emit("v", lambda e, k=k, bx=bx: e.tensor_tensor(
                    bx[:, k * TC:(k + 1) * TC], bx[:, k * TC:(k + 1) * TC],
                    P[2][:], ALU.subtract), waits=[tmb])
                w_xn = S.emit("v", lambda e, k=k, bx=bx: e.tensor_mul(
                    bx[:, k * TC:(k + 1) * TC], bx[:, k * TC:(k + 1) * TC],
                    P[3][:]), waits=[trb, ts_])
            pdeps[2].append(w_xn)
            pdeps[3].append(w_xn)
            w0 = pacq(0)
            tq = None
            for k in range(DC):
                ss, sap, sw = rb_sq.acquire()
                tsq = S.emit("s", lambda e, sap=sap, k=k, bx=bx: e.activation(
                    sap, bx[:, k * TC:(k + 1) * TC], AF.Square),
                    waits=sw + [w_xn])
                tq = S.emit("t", lambda e, sap=sap, k=k: e.matmul(
                    P[0][0:1, :], b_onesd[:], sap,
                    start=(k == 0), stop=(k == DC - 1)),
                    waits=[tsq] + (w0 if k == 0 else []))
                rb_sq.used(ss, tq)
            rrow = b_rows[0:1, 2 * TC:3 * TC]
            t3 = S.emit("s", lambda e: e.activation(rrow, P[0][0:1, :], AF.Sqrt,
                                                    bias=b_cst[0:1, 0:1]),
                        waits=[tq] + WPL)
            pdeps[0].append(t3)
            t4 = S.emit("v", lambda e: e.reciprocal(rrow, rrow), waits=[t3])
            trr = S.emit("t", lambda e: e.matmul(P[1][:], b_ones1[:], rrow,
                                                 start=True, stop=True),
                         waits=pacq(1) + [t4])
            w_h = None
            for k in range(DC):
                w_h = S.emit("v", lambda e, k=k, bx=bx: e.scalar_tensor_tensor(
                    bh[:, k * TC:(k + 1) * TC], bx[:, k * TC:(k + 1) * TC],
                    b_nw[:, k:k + 1], P[1][:], ALU.mult, ALU.mult),
                    waits=[trr, w_xn] + WPL + xdep["bh"])
            pdeps[1].append(w_h)

            tsil_u = [None] * 8
            t_inproj_last = None
            for mg4 in range(4):
                chain_w = [pacq(q) for q in range(4)]
                tags4 = [None] * 4
                for k in range(DC):
                    ws, wap, ww = rb_w.acquire()
                    tw = S.emit("sy", lambda e, wap=wap, k=k, mg4=mg4: e.dma_start(
                        out=wap, in_=w_in[k * 128:(k + 1) * 128,
                                         mg4 * 512:(mg4 + 1) * 512]),
                        waits=ww)
                    for q in range(4):
                        tags4[q] = S.emit("t", lambda e, wap=wap, q=q, k=k:
                                          e.matmul(
                            P[q][:], wap[:, q * 128:(q + 1) * 128],
                            bh[:, k * TC:(k + 1) * TC],
                            start=(k == 0), stop=(k == DC - 1)),
                            waits=[tw, w_h] + (chain_w[q] if k == 0 else []))
                        rb_w.used(ws, tags4[q])
                t_inproj_last = tags4[3]
                if mg4 < 2:
                    for q in range(4):
                        mm = mg4 * 4 + q
                        cs, cap, cw = rb_cb.acquire()
                        th0 = S.emit("s", lambda e, cap=cap, mm=mm: e.activation(
                            cap[:, 0:3], b_halo[:, mm * 3:(mm + 1) * 3],
                            AF.Copy), waits=cw + [tags4[q], WPV])
                        t1 = S.emit("s", lambda e, cap=cap, q=q: e.activation(
                            cap[:, 3:3 + TC], P[q][:], AF.Copy), waits=[th0])
                        pdeps[q].append(t1)
                        th2 = S.emit("s", lambda e, cap=cap, mm=mm: e.activation(
                            b_halo[:, mm * 3:(mm + 1) * 3], cap[:, TC:TC + 3],
                            AF.Copy), waits=[t1])
                        acc = bdu[:, mm * TC:(mm + 1) * TC]
                        tconv = S.emit("v", lambda e, cap=cap, acc=acc, mm=mm:
                               e.tensor_scalar(
                                   acc, cap[:, 0:TC],
                                   b_cvw[:, mm * 4:mm * 4 + 1], None, ALU.mult),
                               waits=WPL + [th2])
                        for kk in range(1, 4):
                            tconv = S.emit(
                                "v", lambda e, cap=cap, acc=acc, mm=mm, kk=kk:
                                e.scalar_tensor_tensor(
                                    acc, cap[:, kk:kk + TC],
                                    b_cvw[:, mm * 4 + kk:mm * 4 + kk + 1],
                                    acc, ALU.mult, ALU.add), waits=[tconv])
                        rb_cb.used(cs, tconv)
                        tsil_u[mm] = S.emit("s", lambda e, acc=acc, mm=mm:
                                            e.activation(
                            bu[:, mm * TC:(mm + 1) * TC], acc, AF.Silu,
                            bias=b_cvb[:, mm:mm + 1]),
                            waits=[tconv] + WPL + xdep["bu"])
                else:
                    for q in range(4):
                        mm = (mg4 - 2) * 4 + q
                        t = S.emit("s", lambda e, mm=mm, q=q: e.activation(
                            bsr[:, mm * TC:(mm + 1) * TC], P[q][:], AF.Silu),
                            waits=[tags4[q]] + xdep["bsr"])
                        pdeps[q].append(t)
            w_u = S.now("s")

            wxp = pacq(4)
            txd = None
            for m in range(MI):
                txd = S.emit("t", lambda e, m=m: e.matmul(
                    P[4][0:XD, :], b_wxp[:, m * XD:(m + 1) * XD],
                    bu[:, m * TC:(m + 1) * TC],
                    start=(m == 0), stop=(m == MI - 1)),
                    waits=[w_u] + WPL + (wxp if m == 0 else []))
            tev = S.emit("v", lambda e: e.tensor_copy(xdl[:], P[4][0:XD, :]),
                         waits=[txd] + xdep["xdl"])
            pdeps[4].append(tev)
            td1 = S.emit("g", lambda e, tau=tau: e.dma_start(out=xd_in[tau],
                                                             in_=xdl[:]),
                         waits=[tev])
            tcc = S.emit("g", lambda e, tau=tau: e.collective_compute(
                "AllGather", ALU.bypass, replica_groups=RG,
                ins=[xd_in[tau]], outs=[xd_out[tau]]),
                waits=[td1], sem="cc", n=1)
            td2 = S.emit("g", lambda e, tau=tau: e.dma_start(
                out=xdf[:], in_=xd_out[tau][0]), waits=[tcc] + xdep["xdf"])
            td3 = S.emit("g", lambda e, tau=tau: e.dma_start(
                out=xdf2[:], in_=xd_out[tau][1]), waits=[tcc])
            tsum = S.emit("v", lambda e: e.tensor_add(xdf[:], xdf[:], xdf2[:]),
                          waits=[td2, td3])
            td4 = S.emit("g", lambda e, tau=tau: e.dma_start(out=xdf_d[tau],
                                                             in_=xdf[:]),
                         waits=[tsum])

            w_dl = []
            for m in range(MI):
                pi = 5 + (m % 2)
                tdt = S.emit("t", lambda e, m=m, pi=pi: e.matmul(
                    P[pi][:], b_wdt[:, m * 128:(m + 1) * 128], xdf[0:DTR, :],
                    start=True, stop=True),
                    waits=[tsum] + WPL + pacq(pi))
                ss, sap, sw = rb_sq.acquire()
                te1 = S.emit("s", lambda e, sap=sap, m=m, pi=pi: e.activation(
                    sap, P[pi][:], AF.Exp, bias=b_dtb[:, m:m + 1]),
                    waits=[tdt] + WPL + sw)
                pdeps[pi].append(te1)
                tln = S.emit("s", lambda e, sap=sap, m=m: e.activation(
                    bdl[:, m * TC:(m + 1) * TC], sap, AF.Ln,
                    bias=b_cst[:, 1:2]), waits=WPL + [te1] + xdep["bdl"])
                rb_sq.used(ss, tln)
                tdu = S.emit("v", lambda e, m=m: e.tensor_mul(
                    bdu[:, m * TC:(m + 1) * TC], bdl[:, m * TC:(m + 1) * TC],
                    bu[:, m * TC:(m + 1) * TC]), waits=[tln])
                w_dl.append((tln, tdu))

            for n in range(NST):
                bs_, bB, wB = rb_bcB.acquire()
                tB = S.emit("sy", lambda e, bB=bB, n=n, tau=tau: e.dma_start(
                    out=bB, in_=xdf_d[tau][DTR + n:DTR + n + 1, :]
                    .to_broadcast((128, TC))), waits=wB + [td4])
                cs_, bC, wC = rb_bcC.acquire()
                tC = S.emit("sy", lambda e, bC=bC, n=n, tau=tau: e.dma_start(
                    out=bC, in_=xdf_d[tau][DTR + NST + n:DTR + NST + n + 1, :]
                    .to_broadcast((128, TC))), waits=wC + [td4])
                for m in range(MI):
                    tln, tdu = w_dl[m]
                    col = m * NST + n
                    es, eap, ew = rb_exp.acquire()
                    texp = S.emit("s", lambda e, eap=eap, m=m, col=col:
                                  e.activation(
                        eap, bdl[:, m * TC:(m + 1) * TC], AF.Exp,
                        scale=b_A[:, col:col + 1]),
                        waits=ew + [tln, WPV])
                    ds_, dap, dw = rb_dbu.acquire()
                    tdbu = S.emit("v", lambda e, dap=dap, bB=bB, m=m: e.tensor_mul(
                        dap, bdu[:, m * TC:(m + 1) * TC], bB),
                        waits=dw + [tdu, tB])
                    rb_bcB.used(bs_, tdbu)
                    tscan = S.emit("v", lambda e, eap=eap, dap=dap, col=col:
                                   e.tensor_tensor_scan(
                                       eap, eap, dap, b_state[:, col:col + 1],
                                       ALU.mult, ALU.add),
                                   waits=[texp, tdbu, state_tags.get(col)])
                    rb_dbu.used(ds_, tscan)
                    state_tags[col] = S.emit(
                        "v", lambda e, eap=eap, col=col: e.tensor_copy(
                            b_state[:, col:col + 1], eap[:, TC - 1:TC]),
                        waits=[tscan])
                    if n == 0:
                        ty = S.emit("v", lambda e, eap=eap, bC=bC, m=m:
                                    e.tensor_mul(
                            by[:, m * TC:(m + 1) * TC], eap, bC),
                            waits=[tC, tscan, ty_tags.get(m)])
                        rb_exp.used(es, ty)
                        rb_bcC.used(cs_, ty)
                    else:
                        ds2, dap2, dw2 = rb_dbu.acquire()
                        tsc = S.emit("v", lambda e, eap=eap, dap2=dap2, bC=bC:
                                     e.tensor_mul(dap2, eap, bC),
                                     waits=dw2 + [tC, tscan])
                        rb_exp.used(es, tsc)
                        rb_bcC.used(cs_, tsc)
                        ty = S.emit("v", lambda e, dap2=dap2, m=m: e.tensor_add(
                            by[:, m * TC:(m + 1) * TC],
                            by[:, m * TC:(m + 1) * TC], dap2),
                            waits=[tsc, ty_tags.get(m)])
                        rb_dbu.used(ds2, ty)
                    ty_tags[m] = ty

            w_y = [None] * MI
            for m in range(MI):
                typ = S.emit("v", lambda e, m=m: e.scalar_tensor_tensor(
                    by[:, m * TC:(m + 1) * TC], bu[:, m * TC:(m + 1) * TC],
                    b_dv[:, m:m + 1], by[:, m * TC:(m + 1) * TC],
                    ALU.mult, ALU.add), waits=WPL + [ty_tags.get(m)])
                w_y[m] = S.emit("v", lambda e, m=m: e.tensor_mul(
                    by[:, m * TC:(m + 1) * TC], by[:, m * TC:(m + 1) * TC],
                    bsr[:, m * TC:(m + 1) * TC]), waits=[typ])

            w_pout = []
            t_outproj_last = None
            for mog in range(2):
                chain_w = [pacq(4 + q) for q in range(4)]
                tags4 = [None] * 4
                for m in range(MI):
                    ws, wap, ww = rb_w.acquire()
                    tw = S.emit("sy", lambda e, wap=wap, m=m, mog=mog: e.dma_start(
                        out=wap, in_=w_out[m * 128:(m + 1) * 128,
                                          mog * 512:(mog + 1) * 512]),
                        waits=ww)
                    for q in range(4):
                        tags4[q] = S.emit("t", lambda e, wap=wap, q=q, m=m:
                                          e.matmul(
                            P[4 + q][:], wap[:, q * 128:(q + 1) * 128],
                            by[:, m * TC:(m + 1) * TC],
                            start=(m == 0), stop=(m == MI - 1)),
                            waits=[tw, w_y[m]] + (chain_w[q] if m == 0 else []))
                        rb_w.used(ws, tags4[q])
                t_outproj_last = tags4[3]
                for q in range(4):
                    mo = mog * 4 + q
                    t = S.emit("v", lambda e, mo=mo, q=q, bx=bx: e.scalar_tensor_tensor(
                        bdl[:, mo * TC:(mo + 1) * TC],
                        bx[:, mo * TC:(mo + 1) * TC],
                        0.5, P[4 + q][:], ALU.mult, ALU.add),
                        waits=[tags4[q]])
                    pdeps[4 + q].append(t)
                    w_pout.append(t)
            tpo = S.emit("g", lambda e, tau=tau: e.dma_start(
                out=p_in[tau].rearrange("(dc p) c -> p dc c", p=128),
                in_=bdl[:].rearrange("p (dc c) -> p dc c", c=TC)),
                waits=w_pout)
            S.emit("g", lambda e, tau=tau: e.collective_compute(
                "AllGather", ALU.bypass, replica_groups=RG,
                ins=[p_in[tau]], outs=[p_out[tau]]),
                waits=[tpo], sem="cc", n=1)
            xdep[bxkey] = w_pout
            xdep["bh"] = [t_inproj_last]
            xdep["bu"] = [S.now("v")]
            xdep["bsr"] = [w_y[7]]
            xdep["bdl"] = [tpo]
            xdep["xdl"] = [td1]
            xdep["xdf"] = [tdt]
            for m in range(MI):
                ty_tags[m] = t_outproj_last

        _mctx.close()
        bXn = sb("bXn", [128, DC * PT])
        racc = sb("racc", [128, DC * PT])
        att0 = sb("att0", [128, DC * TC])
        att1 = sb("att1", [128, DC * TC])
        bW1 = sb("bW1", [128, 2 * 8 * HF])
        bW2 = sb("bW2", [128, HC * D])
        rb_w1 = Rot([bW1[:, 0:8 * HF], bW1[:, 8 * HF:2 * 8 * HF]])
        bHb = sb("bHb", [128, HC * TC])
        cwbc = sb("cwbc", [128, 2 * TC])
        blg = sb("blg", [NE, TC])
        btl = sb("btl", [128, 2 * NE])
        bts = sb("bts", [128, 8 * NE])
        bcw = sb("bcw", [NE, PT])
        rb_cwb = Rot([cwbc[:, 0:TC], cwbc[:, TC:2 * TC]])

        S.barrier()

        ffn_dep = {"att0": [], "att1": [], "bXn": [], "racc": [], "bcw": []}
        for pp in range(NPASS):
            w_x1 = None
            for hh in range(2):
                cc_i = 2 * pp + hh
                wait_cc = ("cc", 2 * (cc_i + 1))
                ta0 = S.emit("sy", lambda e, cc_i=cc_i: e.dma_start(
                    out=att0[:].rearrange("p (dc c) -> p dc c", c=TC),
                    in_=p_out[cc_i][0].rearrange("(dc p) c -> p dc c", p=128)),
                    waits=[wait_cc, S.now("t"), S.now("v"), S.now("s")]
                    + ffn_dep["att0"])
                ta1 = S.emit("sy", lambda e, cc_i=cc_i: e.dma_start(
                    out=att1[:].rearrange("p (dc c) -> p dc c", c=TC),
                    in_=p_out[cc_i][1].rearrange("(dc p) c -> p dc c", p=128)),
                    waits=[wait_cc, S.now("t"), S.now("v")] + ffn_dep["att1"])
                tadd = None
                for k in range(DC):
                    tadd = S.emit("v", lambda e, k=k: e.tensor_add(
                        att0[:, k * TC:(k + 1) * TC],
                        att0[:, k * TC:(k + 1) * TC],
                        att1[:, k * TC:(k + 1) * TC]), waits=[ta0, ta1])
                tmb, trb = emit_ln_stats(att0, TC, 0, 1, 2, 3, [tadd])
                wxf = []
                for k in range(DC):
                    wxf.append(S.emit("sy", lambda e, k=k, cc_i=cc_i: e.dma_start(
                        out=att1[:, k * TC:(k + 1) * TC],
                        in_=xT[k * 128:(k + 1) * 128,
                               cc_i * TC:(cc_i + 1) * TC]), waits=[tadd]))
                w_x1c = []
                for k in range(DC):
                    ts_ = S.emit("v", lambda e, k=k: e.tensor_tensor(
                        att0[:, k * TC:(k + 1) * TC],
                        att0[:, k * TC:(k + 1) * TC],
                        P[2][:], ALU.subtract), waits=[tmb])
                    tm_ = S.emit("v", lambda e, k=k: e.tensor_mul(
                        att0[:, k * TC:(k + 1) * TC],
                        att0[:, k * TC:(k + 1) * TC], P[3][:]),
                        waits=[trb, ts_])
                    t_ = S.emit("v", lambda e, k=k: e.tensor_add(
                        att0[:, k * TC:(k + 1) * TC],
                        att0[:, k * TC:(k + 1) * TC],
                        att1[:, k * TC:(k + 1) * TC]), waits=wxf + [tm_])
                    w_x1c.append(t_)
                pdeps[2].append(w_x1c[-1])
                pdeps[3].append(w_x1c[-1])
                tx1d = S.emit("g", lambda e, pp=pp, hh=hh: e.dma_start(
                    out=x1_d[pp].rearrange("(dc p) c -> p dc c", p=128)
                    [:, :, hh * TC:(hh + 1) * TC],
                    in_=att0[:].rearrange("p (dc c) -> p dc c", c=TC)),
                    waits=w_x1c)
                tmb, trb = emit_ln_stats(att0, TC, 0, 1, 2, 3, w_x1c)
                for k in range(DC):
                    ts_ = S.emit("v", lambda e, k=k: e.tensor_tensor(
                        att0[:, k * TC:(k + 1) * TC],
                        att0[:, k * TC:(k + 1) * TC],
                        P[2][:], ALU.subtract), waits=[tmb, tx1d])
                    w_x1 = S.emit("v", lambda e, k=k, hh=hh: e.tensor_mul(
                        bXn[:, k * PT + hh * TC:k * PT + (hh + 1) * TC],
                        att0[:, k * TC:(k + 1) * TC], P[3][:]),
                        waits=[trb, ts_] + ffn_dep["bXn"])
                pdeps[2].append(w_x1)
                pdeps[3].append(w_x1)

            for tt in range(2):
                tlg = None
                wlg = pacq(4)
                for k in range(DC):
                    tlg = S.emit("t", lambda e, k=k, tt=tt: e.matmul(
                        P[4][0:NE, :], b_rw[:, k * NE:(k + 1) * NE],
                        bXn[:, k * PT + tt * TC:k * PT + (tt + 1) * TC],
                        start=(k == 0), stop=(k == DC - 1)),
                        waits=[w_x1] + WPL + (wlg if k == 0 else []))
                tb = S.emit("v", lambda e: e.tensor_scalar(
                    blg[:], P[4][0:NE, :], b_rb[:, 0:1], None, ALU.add),
                    waits=[tlg] + WPL)
                pdeps[4].append(tb)
                for g in range(NG):
                    pt_i = 5 if g % 2 == 0 else 6
                    ttr = S.emit("t", lambda e, g=g, pt_i=pt_i: e.transpose(
                        P[pt_i][:, 0:NE], blg[:, g * 128:(g + 1) * 128],
                        b_id15[:]), waits=[tb] + WPL + pacq(pt_i))
                    lt = btl[:, (g % 2) * NE:(g % 2 + 1) * NE]
                    tcp = S.emit("v", lambda e, lt=lt, pt_i=pt_i: e.tensor_copy(
                        lt, P[pt_i][:, 0:NE]), waits=[ttr])
                    pdeps[pt_i].append(tcp)
                    mx, sm_, m1, m2, m3 = (bts[:, i:i + 1] for i in range(5))
                    prb = bts[:, NE:2 * NE]
                    t1b = bts[:, 2 * NE:3 * NE]
                    t2b = bts[:, 3 * NE:4 * NE]
                    mskb = bts[:, 4 * NE:5 * NE]
                    S.emit("v", lambda e, lt=lt, mx=mx: e.reduce_max(
                        mx, lt, axis=AX.X), waits=[tcp])
                    S.emit("v", lambda e, lt=lt, mx=mx: e.tensor_scalar(
                        lt, lt, mx, None, ALU.subtract), waits=[S.now("v")])
                    te = S.emit("s", lambda e, lt=lt: e.activation(
                        lt, lt, AF.Exp), waits=[S.now("v")])
                    S.emit("v", lambda e, lt=lt, sm_=sm_: e.reduce_sum(
                        sm_, lt, axis=AX.X), waits=[te])
                    S.emit("v", lambda e, sm_=sm_: e.reciprocal(sm_, sm_),
                           waits=[S.now("v")])
                    S.emit("v", lambda e, lt=lt, sm_=sm_, prb=prb: e.tensor_scalar(
                        prb, lt, sm_, None, ALU.mult), waits=[S.now("v")])
                    S.emit("v", lambda e, prb=prb, m1=m1: e.reduce_max(
                        m1, prb, axis=AX.X), waits=[S.now("v")])
                    S.emit("v", lambda e, prb=prb, m1=m1, mskb=mskb:
                           e.tensor_scalar(mskb, prb, m1, None, ALU.is_lt),
                           waits=[S.now("v")])
                    S.emit("v", lambda e, prb=prb, mskb=mskb, t1b=t1b:
                           e.tensor_mul(t1b, prb, mskb), waits=[S.now("v")])
                    S.emit("v", lambda e, t1b=t1b, m2=m2: e.reduce_max(
                        m2, t1b, axis=AX.X), waits=[S.now("v")])
                    S.emit("v", lambda e, t1b=t1b, m2=m2, mskb=mskb:
                           e.tensor_scalar(mskb, t1b, m2, None, ALU.is_lt),
                           waits=[S.now("v")])
                    S.emit("v", lambda e, t1b=t1b, mskb=mskb, t2b=t2b:
                           e.tensor_mul(t2b, t1b, mskb), waits=[S.now("v")])
                    S.emit("v", lambda e, t2b=t2b, m3=m3: e.reduce_max(
                        m3, t2b, axis=AX.X), waits=[S.now("v")])
                    S.emit("v", lambda e, prb=prb, m3=m3, mskb=mskb:
                           e.tensor_scalar(mskb, prb, m3, None, ALU.is_ge),
                           waits=[S.now("v")])
                    tcw = S.emit("v", lambda e, lt=lt, prb=prb, mskb=mskb:
                                 e.tensor_mul(lt, prb, mskb),
                                 waits=[S.now("v")])
                    ttc = S.emit("t", lambda e, lt=lt: e.transpose(
                        P[7][0:NE, 0:128], lt, b_id128[:]),
                        waits=[tcw] + WPL + pacq(7))
                    tcc2 = S.emit("v", lambda e, g=g, tt=tt: e.tensor_copy(
                        bcw[:, tt * TC + g * 128:tt * TC + (g + 1) * 128],
                        P[7][0:NE, 0:128]), waits=[ttc] + ffn_dep["bcw"])
                    pdeps[7].append(tcc2)
            w_cw = S.now("v")
            tcwd = S.emit("g", lambda e, pp=pp: e.dma_start(out=cw_d[pp],
                                                            in_=bcw[:]),
                          waits=[w_cw])

            racc_tags = {}
            for tt in range(2):
                for dc in range(DC):
                    pi = 2 + (dc % 2)
                    ti = S.emit("t", lambda e, dc=dc, tt=tt, pi=pi: e.matmul(
                        P[pi][:], b_eb2h[:, dc * 128:(dc + 1) * 128],
                        bcw[:, tt * TC:(tt + 1) * TC],
                        start=True, stop=True),
                        waits=[w_cw, WPV] + pacq(pi))
                    t = S.emit("v", lambda e, dc=dc, tt=tt, pi=pi: e.tensor_scalar(
                        racc[:, dc * PT + tt * TC:dc * PT + (tt + 1) * TC],
                        P[pi][:], b_sb2h[:, dc:dc + 1], None, ALU.add),
                        waits=[ti, WPV] + ffn_dep["racc"])
                    pdeps[pi].append(t)
                    racc_tags[(tt, dc)] = t

            for e_i in range(NE + 1):
                w1s, w1ap, w1w = rb_w1.acquire()
                src1 = (e_w1[e_i] if e_i < NE else sw1)
                src2 = (e_w2[e_i] if e_i < NE else sw2)
                tw1 = S.emit("sy", lambda e, w1ap=w1ap, src1=src1: e.dma_start(
                    out=w1ap.rearrange("p (k h) -> p k h", h=HF),
                    in_=src1.rearrange("(k p) h -> p k h", p=128)),
                    waits=w1w)
                tw2 = S.emit("sy", lambda e, src2=src2: e.dma_start(
                    out=bW2[:].rearrange("p (k d) -> p k d", d=D),
                    in_=src2.rearrange("(k p) d -> p k d", p=128)),
                    waits=[S.now("t")])
                for tt in range(2):
                    w_bh = []
                    for hc in range(HC):
                        pi = hc % 2
                        tH = None
                        wH = pacq(pi)
                        for k in range(DC):
                            tH = S.emit("t", lambda e, w1ap=w1ap, hc=hc, k=k,
                                        tt=tt, pi=pi: e.matmul(
                                P[pi][:],
                                w1ap[:, k * HF + hc * 128:k * HF + (hc + 1) * 128],
                                bXn[:, k * PT + tt * TC:k * PT + (tt + 1) * TC],
                                start=(k == 0), stop=(k == DC - 1)),
                                waits=[tw1, w_x1] + (wH if k == 0 else []))
                            rb_w1.used(w1s, tH)
                        if e_i < NE:
                            bias = b_eb1[:, e_i * HC + hc:e_i * HC + hc + 1]
                            scr = att0[:, hc * TC:(hc + 1) * TC]
                            tr = S.emit("s", lambda e, scr=scr, pi=pi, bias=bias:
                                        e.activation(scr, P[pi][:], AF.Relu,
                                                     bias=bias),
                                        waits=[tH] + WPL)
                            pdeps[pi].append(tr)
                            cbs, cbap, cbw = rb_cwb.acquire()
                            tbc = S.emit("sy", lambda e, cbap=cbap, e_i=e_i,
                                         tt=tt, pp=pp: e.dma_start(
                                out=cbap,
                                in_=cw_d[pp][e_i:e_i + 1, tt * TC:(tt + 1) * TC]
                                .to_broadcast((128, TC))),
                                waits=cbw + [tcwd])
                            tm = S.emit("v", lambda e, scr=scr, cbap=cbap, hc=hc:
                                        e.tensor_mul(
                                bHb[:, hc * TC:(hc + 1) * TC], scr, cbap),
                                waits=[tr, tbc])
                            rb_cwb.used(cbs, tm)
                            w_bh.append(tm)
                        else:
                            bias = b_sb1[:, hc:hc + 1]
                            tr = S.emit("s", lambda e, hc=hc, pi=pi, bias=bias:
                                        e.activation(
                                bHb[:, hc * TC:(hc + 1) * TC], P[pi][:],
                                AF.Relu, bias=bias), waits=[tH] + WPL)
                            pdeps[pi].append(tr)
                            w_bh.append(tr)
                    for dc in range(DC):
                        pi = 2 + (dc % 2)
                        tR = None
                        wR = pacq(pi)
                        for hk in range(HC):
                            tR = S.emit("t", lambda e, dc=dc, hk=hk, pi=pi:
                                        e.matmul(
                                P[pi][:],
                                bW2[:, hk * D + dc * 128:hk * D + (dc + 1) * 128],
                                bHb[:, hk * TC:(hk + 1) * TC],
                                start=(hk == 0), stop=(hk == HC - 1)),
                                waits=[tw2] + w_bh + (wR if hk == 0 else []))
                        t = S.emit("v", lambda e, dc=dc, tt=tt, pi=pi:
                                   e.scalar_tensor_tensor(
                            racc[:, dc * PT + tt * TC:dc * PT + (tt + 1) * TC],
                            P[pi][:], 1.0,
                            racc[:, dc * PT + tt * TC:dc * PT + (tt + 1) * TC],
                            ALU.mult, ALU.add),
                            waits=[tR, racc_tags.get((tt, dc))])
                        pdeps[pi].append(t)
                        racc_tags[(tt, dc)] = t
            w_racc = S.now("v")

            trt = S.emit("g", lambda e, pp=pp: e.dma_start(
                out=rt_in[pp].rearrange("(dc p) c -> p dc c", p=128),
                in_=racc[:].rearrange("p (dc c) -> p dc c", c=PT)),
                waits=[w_racc])
            trc = S.emit("g", lambda e, pp=pp: e.collective_compute(
                "AllGather", ALU.bypass, replica_groups=RG,
                ins=[rt_in[pp]], outs=[rt_out[pp]]),
                waits=[trt], sem="cc", n=1)

            tout = None
            for hh in range(2):
                tf0 = S.emit("sy", lambda e, pp=pp, hh=hh: e.dma_start(
                    out=att0[:].rearrange("p (dc c) -> p dc c", c=TC),
                    in_=rt_out[pp][0].rearrange("(dc p) c -> p dc c", p=128)
                    [:, :, hh * TC:(hh + 1) * TC]),
                    waits=[trc, S.now("s"), S.now("t"), S.now("v"), tout])
                tf1 = S.emit("sy", lambda e, pp=pp, hh=hh: e.dma_start(
                    out=att1[:].rearrange("p (dc c) -> p dc c", c=TC),
                    in_=rt_out[pp][1].rearrange("(dc p) c -> p dc c", p=128)
                    [:, :, hh * TC:(hh + 1) * TC]),
                    waits=[trc, S.now("t"), S.now("v"), tout])
                tadd = None
                for k in range(DC):
                    tadd = S.emit("v", lambda e, k=k: e.tensor_add(
                        att0[:, k * TC:(k + 1) * TC],
                        att0[:, k * TC:(k + 1) * TC],
                        att1[:, k * TC:(k + 1) * TC]), waits=[tf0, tf1])
                tmb, trb = emit_ln_stats(att0, TC, 0, 1, 2, 3, [tadd])
                tx1l = S.emit("sy", lambda e, pp=pp, hh=hh: e.dma_start(
                    out=att1[:].rearrange("p (dc c) -> p dc c", c=TC),
                    in_=x1_d[pp].rearrange("(dc p) c -> p dc c", p=128)
                    [:, :, hh * TC:(hh + 1) * TC]), waits=[tadd])
                w_fin = []
                for k in range(DC):
                    ts_ = S.emit("v", lambda e, k=k: e.tensor_tensor(
                        att0[:, k * TC:(k + 1) * TC],
                        att0[:, k * TC:(k + 1) * TC],
                        P[2][:], ALU.subtract), waits=[tmb])
                    tm_ = S.emit("v", lambda e, k=k: e.tensor_mul(
                        att0[:, k * TC:(k + 1) * TC],
                        att0[:, k * TC:(k + 1) * TC], P[3][:]),
                        waits=[trb, ts_])
                    t = S.emit("v", lambda e, k=k: e.tensor_add(
                        att1[:, k * TC:(k + 1) * TC],
                        att0[:, k * TC:(k + 1) * TC],
                        att1[:, k * TC:(k + 1) * TC]),
                        waits=[tm_, tx1l])
                    w_fin.append(t)
                pdeps[2].append(w_fin[-1])
                pdeps[3].append(w_fin[-1])
                tout = S.emit("g", lambda e, pp=pp, hh=hh: e.dma_start(
                    out=out.rearrange("(dc p) c -> p dc c", p=128)
                    [:, :, (2 * pp + hh) * TC:(2 * pp + hh + 1) * TC],
                    in_=att1[:].rearrange("p (dc c) -> p dc c", c=TC)),
                    waits=w_fin)
            ffn_dep["att0"] = [S.now("v")]
            ffn_dep["att1"] = [tout]
            ffn_dep["bXn"] = [S.now("t")]
            ffn_dep["racc"] = [trt]
            ffn_dep["bcw"] = [tcwd, S.now("t")]

        S.emit("g", lambda e: e.nop(),
               waits=[S.now("dg0"), S.now("dg1")], sem=None)

        def replay(h, name):
            for rw, fn, sem, n in S.ops[name]:
                for sname, v in rw:
                    h.wait_ge(SEM[sname], v)
                r = fn(h)
                if sem is not None:
                    r.then_inc(SEM[sem], n)

        @block.sync
        def _(sync):
            replay(sync, "sy")

        @block.tensor
        def _(tensor):
            replay(tensor, "t")

        @block.vector
        def _(vector):
            replay(vector, "v")

        @block.scalar
        def _(scalar):
            replay(scalar, "s")

        @block.gpsimd
        def _(gpsimd):
            replay(gpsimd, "g")

    return nc


def _prep_core_inputs(inp, b, j):
    f32 = np.float32
    x = np.asarray(inp["L0_attn_x"], dtype=f32)
    xT = np.ascontiguousarray(x[b].T)

    w_inT = np.asarray(inp["L0_attn_in_proj_w"], dtype=f32).T
    w_in = np.ascontiguousarray(np.concatenate(
        [w_inT[:, j * DIH:(j + 1) * DIH],
         w_inT[:, DI + j * DIH:DI + (j + 1) * DIH]], axis=1))

    w_xp = np.ascontiguousarray(
        np.asarray(inp["L0_attn_x_proj_w"], dtype=f32).T[j * DIH:(j + 1) * DIH])
    w_dt = np.ascontiguousarray(
        np.asarray(inp["L0_attn_dt_proj_w"], dtype=f32).T[:, j * DIH:(j + 1) * DIH])

    def pack_pp(vec, nt):
        return np.ascontiguousarray(vec.reshape(nt, 128).T)

    dt_b = pack_pp(np.asarray(inp["L0_attn_dt_proj_b"], dtype=f32)
                   [j * DIH:(j + 1) * DIH], MI)
    cvw = np.asarray(inp["L0_attn_conv_w"], dtype=f32)[:, 0, :][j * DIH:(j + 1) * DIH]
    cv_w = np.ascontiguousarray(
        cvw.reshape(MI, 128, 4).transpose(1, 0, 2).reshape(128, MI * 4))
    cv_b = pack_pp(np.asarray(inp["L0_attn_conv_b"], dtype=f32)
                   [j * DIH:(j + 1) * DIH], MI)
    al = np.asarray(inp["L0_attn_A_log"], dtype=f32)[j * DIH:(j + 1) * DIH]
    a_log = np.ascontiguousarray(
        al.reshape(MI, 128, NST).transpose(1, 0, 2).reshape(128, MI * NST))
    d_vec = pack_pp(np.asarray(inp["L0_attn_D"], dtype=f32)[j * DIH:(j + 1) * DIH], MI)
    nw = pack_pp(np.asarray(inp["L0_attn_norm_w"], dtype=f32), DC)
    w_out = np.ascontiguousarray(
        np.asarray(inp["L0_attn_out_proj_w"], dtype=f32).T[j * DIH:(j + 1) * DIH])

    r_w = np.ascontiguousarray(np.asarray(inp["L0_ffn_router_W"], dtype=f32))
    r_b = np.ascontiguousarray(
        np.asarray(inp["L0_ffn_router_b"], dtype=f32).reshape(NE, 1))
    sw1 = np.ascontiguousarray(
        np.asarray(inp["L0_ffn_sW1"], dtype=f32)[:, j * HF:(j + 1) * HF])
    sb1 = pack_pp(np.asarray(inp["L0_ffn_sb1"], dtype=f32)[j * HF:(j + 1) * HF], HC)
    sw2 = np.ascontiguousarray(
        np.asarray(inp["L0_ffn_sW2"], dtype=f32)[j * HF:(j + 1) * HF])
    sb2 = pack_pp(np.asarray(inp["L0_ffn_sb2"], dtype=f32), DC)
    e_w1 = np.ascontiguousarray(
        np.asarray(inp["L0_ffn_eW1"], dtype=f32)[:, :, j * HF:(j + 1) * HF])
    eb1 = np.asarray(inp["L0_ffn_eb1"], dtype=f32)[:, j * HF:(j + 1) * HF]
    e_b1 = np.ascontiguousarray(
        eb1.reshape(NE, HC, 128).transpose(2, 0, 1).reshape(128, NE * HC))
    e_w2 = np.ascontiguousarray(
        np.asarray(inp["L0_ffn_eW2"], dtype=f32)[:, j * HF:(j + 1) * HF])
    e_b2 = np.ascontiguousarray(np.asarray(inp["L0_ffn_eb2"], dtype=f32))

    return dict(
        xT=xT, w_in=w_in, w_xp=w_xp, w_dt=w_dt, dt_b=dt_b, cv_w=cv_w, cv_b=cv_b,
        a_log=a_log, d_vec=d_vec, nw=nw, w_out=w_out, r_w=r_w, r_b=r_b,
        sw1=sw1, sb1=sb1, sw2=sw2, sb2=sb2, e_w1=e_w1, e_b1=e_b1, e_w2=e_w2,
        e_b2=e_b2,
        ones1=np.ones((1, 128), dtype=f32),
        onesd=np.full((128, 1), 1.0 / D, dtype=f32),
        id15=np.eye(NE, dtype=f32),
        id128=np.eye(128, dtype=f32),
        cst=np.stack([np.full(128, EPS, dtype=f32),
                      np.ones(128, dtype=f32)], axis=1),
    )


def kernel(**inp):
    TOK = np.asarray(inp["L0_attn_x"]).shape[1]
    nc = build_nc(TOK=TOK, TC=512)
    in_maps = [_prep_core_inputs(inp, c // 2, c % 2) for c in range(8)]
    res = run_bass_kernel_spmd(nc, in_maps, core_ids=list(range(8)))
    outs = [np.asarray(res.results[2 * b]["out"]).T for b in range(4)]
    return np.stack(outs).astype(np.float32)
